# version 31
# baseline (speedup 1.0000x reference)
import contextlib
import numpy as np
import ml_dtypes
import concourse.bass as bass
import concourse.mybir as mybir
from concourse.bass_utils import run_bass_kernel_spmd

F32 = mybir.dt.float32
BF16 = mybir.dt.bfloat16
I8 = mybir.dt.int8
ALU = mybir.AluOpType
AF = mybir.ActivationFunctionType
AX = mybir.AxisListType

LN_EPS = 1e-5
Q_EPS = 1e-7


class Prog:
    ENGINES = ("sync", "tensor", "vector", "scalar", "gpsimd")

    def __init__(self, sem_names):
        self.ev = {e: [] for e in self.ENGINES}
        self.cnt = {k: 0 for k in sem_names}
        self.hw = {e: {} for e in self.ENGINES}

    def do(self, eng, fn, after=(), sem=None, n=1):
        waits = {}
        for tok in after:
            if tok is None:
                continue
            s, v = tok
            if waits.get(s, 0) < v:
                waits[s] = v
        tok = None
        if sem is not None:
            self.cnt[sem] += n
            tok = (sem, self.cnt[sem])
        self.ev[eng].append((fn, waits, sem, n))
        return tok

    def now(self, sem):
        return (sem, self.cnt[sem])

    def barrier_tokens(self):
        return [(s, v) for s, v in self.cnt.items() if v > 0]

    def flush(self, nc, sems):
        with nc.Block() as block:
            for eng_name in self.ENGINES:
                events = self.ev[eng_name]
                if not events:
                    continue

                def make(eng_name, events):
                    def body(eng):
                        hww = self.hw[eng_name]
                        for fn, waits, sem, n in events:
                            for s, v in waits.items():
                                if hww.get(s, 0) < v:
                                    eng.wait_ge(sems[s], v)
                                    hww[s] = v
                            if fn is None:
                                continue
                            instr = fn(eng)
                            if sem is not None:
                                instr.then_inc(sems[sem], n)
                    return body

                getattr(block, eng_name)(make(eng_name, events))
        self.ev = {e: [] for e in self.ENGINES}


def build_nc(T=4096, C=2048, H=16, detect_races=False):
    D = 128
    NT = T // 128
    KC = C // 128
    NS = NT // 2
    TOWN = NS * 128
    NQC = TOWN // 512
    NCC = C // 512
    SCALE = float(D) ** -0.5

    nc = bass.Bass("TRN2", target_bir_lowering=False,
                   detect_race_conditions=detect_races,
                   dynamic_dma_scratch_size=8192)

    x_seq = nc.dram_tensor("x_seq", [T, C], F32, kind="ExternalInput").ap()
    wq_r = nc.dram_tensor("wq_r", [C, H * D], F32, kind="ExternalInput").ap()
    wk_r = nc.dram_tensor("wk_r", [C, H * D], F32, kind="ExternalInput").ap()
    wv_r = nc.dram_tensor("wv_r", [C, H * D], F32, kind="ExternalInput").ap()
    wo_in = nc.dram_tensor("wo_in", [C, C], F32, kind="ExternalInput").ap()
    bqk_cols = nc.dram_tensor("bqk_cols", [D, 2 * H], F32, kind="ExternalInput").ap()
    bv_row = nc.dram_tensor("bv_row", [1, H * D], F32, kind="ExternalInput").ap()
    bo_row = nc.dram_tensor("bo_row", [1, C], F32, kind="ExternalInput").ap()
    w8 = nc.dram_tensor("w8", [C, C], I8, kind="ExternalInput").ap()
    wsc_b = nc.dram_tensor("wsc_b", [128, C], F32, kind="ExternalInput").ap()
    fbias_b = nc.dram_tensor("fbias_b", [128, C], F32, kind="ExternalInput").ap()
    masks_in = nc.dram_tensor("masks_in", [128, NS * 2 * 128], BF16,
                              kind="ExternalInput").ap()
    out_t = nc.dram_tensor("out", [TOWN, C], F32, kind="ExternalOutput").ap()

    xln_h = nc.dram_tensor("xln_h", [T, C], BF16).ap()
    wq_bf = nc.dram_tensor("wq_bf", [C, H * D], BF16).ap()
    wk_bf = nc.dram_tensor("wk_bf", [C, H * D], BF16).ap()
    wo_bf = nc.dram_tensor("wo_bf", [C, C], BF16).ap()
    w8_bf = nc.dram_tensor("w8_bf", [C, C], BF16).ap()
    v_nat = nc.dram_tensor("v_nat", [T, H * D], BF16).ap()
    yt_hb = nc.dram_tensor("yt_hb", [H, 128, TOWN], BF16).ap()
    q8_h = nc.dram_tensor("q8_h", [TOWN, C], BF16).ap()
    x1_h = nc.dram_tensor("x1_h", [TOWN, C], F32).ap()

    SEM_NAMES = ["tt", "vv", "ss", "gg", "gw", "gv",
                 "c1", "c2", "xi0", "xi1", "xi2", "xi3", "xs0", "xs1", "xt", "vn0", "vn1",
                 "wq0", "wq1", "wk0", "wk1", "vl0", "vl1", "yt",
                 "wo", "yj0", "yj1", "x2a", "x2b", "x1s0", "x1s1", "q80", "q81", "wt",
                 "qj0", "qj1", "xc0", "xc1", "out0", "out1"]
    P = Prog(SEM_NAMES)

    with contextlib.ExitStack() as _semstk:
        sems = {k: _semstk.enter_context(nc.semaphore(k)) for k in SEM_NAMES}

        with (
            nc.sbuf_tensor("xT", [128, KC, T], BF16) as xT,
            nc.sbuf_tensor("ones_f", [1, 128], F32) as ones_f,
            nc.sbuf_tensor("ones_cb", [128, 1], BF16) as ones_cb,
            nc.sbuf_tensor("ones_sq", [128, 128], BF16) as ones_sq,
            nc.sbuf_tensor("negones_f", [1, 128], F32) as negones_f,
            nc.sbuf_tensor("bqk_sb", [128, 2 * H], F32) as bqk_sb,
            nc.sbuf_tensor("bv_sb", [1, H * D], F32) as bv_sb,
        ):
            t_ones = P.do("vector", lambda v: v.memset(ones_f[:, :], 1.0),
                          sem="vv")
            t_ocb = P.do("vector", lambda v: v.memset(ones_cb[:, :], 1.0),
                         sem="vv")
            t_osq = P.do("vector", lambda v: v.memset(ones_sq[:, :], 1.0),
                         sem="vv")
            t_cst = P.do("vector", lambda v: v.memset(negones_f[:, :], -1.0),
                         sem="vv")
            t_bqk = P.do("sync", lambda s: s.dma_start(out=bqk_sb[:, :],
                                                       in_=bqk_cols),
                         sem="c1", n=16)
            t_bv = P.do("sync", lambda s: s.dma_start(out=bv_sb[:, :],
                                                      in_=bv_row),
                        sem="c1", n=16)
            t_bqk = t_bv

            t_wqc = P.do("gpsimd", lambda g: g.dma_start(out=wq_bf, in_=wq_r),
                         sem="gw", n=16)
            t_wkc = P.do("gpsimd", lambda g: g.dma_start(out=wk_bf, in_=wk_r),
                         sem="gw", n=16)
            t_woc = P.do("gpsimd", lambda g: g.dma_start(out=wo_bf, in_=wo_in),
                         sem="gw", n=16)
            t_w8c = P.do("gpsimd", lambda g: g.dma_start(out=w8_bf, in_=w8),
                         sem="gw", n=16)

            with (
                nc.sbuf_tensor("xin", [128, 4, C], F32) as xin,
                nc.sbuf_tensor("xlnb", [128, 2, C], BF16) as xlnb,
                nc.sbuf_tensor("bn6", [128, 2, NCC, 6], F32) as bn6,
                nc.sbuf_tensor("mv", [128, 2, 2], F32) as mv,
                nc.sbuf_tensor("c_ve", [128, 2], F32) as c_ve,
                nc.sbuf_tensor("c_sd", [128, 2], F32) as c_sd,
                nc.sbuf_tensor("c_rs", [128, 2], F32) as c_rs,
                nc.sbuf_tensor("c_nm", [128, 2], F32) as c_nm,
            ):
                t_ld = [None] * NT
                t_st = [None] * NT
                t_ln = [None] * NT
                t_rs_prev = None
                for st in range(NT):
                    b3, b2 = st % 4, st % 2
                    dep = [t_ln[st - 4]] if st >= 4 else []
                    t_ld[st] = P.do("sync",
                                    lambda s, st=st, b3=b3: s.dma_start(
                        out=xin[:, b3, :],
                        in_=x_seq[st * 128:(st + 1) * 128, :]),
                        after=dep, sem="xi%d" % b3, n=16)
                    prev = t_ld[st]
                    dep_bn = [t_ld[st]]
                    if st >= 2:
                        dep_bn.append(t_ln[st - 2])
                    for g in range(NCC):
                        prev = P.do("vector", lambda v, b3=b3, b2=b2, g=g: v.bn_stats(
                            bn6[:, b2, g, :],
                            xin[:, b3, g * 512:(g + 1) * 512]),
                            after=dep_bn + [prev], sem="vv")
                    t_mv = P.do("vector", lambda v, b2=b2: v.bn_aggr(
                        mv[:, b2, :], bn6[:, b2, :, :]),
                                after=[prev], sem="vv")
                    t_ve = P.do("vector", lambda v, b2=b2: v.tensor_scalar(
                        c_ve[:, b2:b2 + 1], mv[:, b2, 1:2], LN_EPS, None,
                        ALU.add), after=[t_mv], sem="vv")
                    dep_sq = [t_ve]
                    if t_rs_prev is not None:
                        dep_sq.append(t_rs_prev)
                    t_sq = P.do("scalar", lambda sc, b2=b2: sc.activation(
                        c_sd[:, b2:b2 + 1], c_ve[:, b2:b2 + 1], AF.Sqrt),
                        after=dep_sq, sem="ss")
                    t_rs = P.do("vector", lambda v, b2=b2: v.reciprocal(
                        c_rs[:, b2:b2 + 1], c_sd[:, b2:b2 + 1]),
                        after=[t_sq], sem="vv")
                    t_rs_prev = t_rs
                    t_nm = P.do("vector", lambda v, b2=b2: v.tensor_scalar(
                        c_nm[:, b2:b2 + 1], mv[:, b2, 0:1], c_rs[:, b2:b2 + 1],
                        -1.0, ALU.mult, ALU.mult), after=[t_rs, t_mv],
                        sem="vv")
                    dep2 = [t_st[st - 2]] if st >= 2 else []
                    t_ln[st] = P.do("scalar", lambda sc, b3=b3, b2=b2: sc.activation(
                        xlnb[:, b2, :], xin[:, b3, :], AF.Identity,
                        bias=c_nm[:, b2:b2 + 1], scale=c_rs[:, b2:b2 + 1]),
                        after=[t_nm, t_ld[st]] + dep2, sem="ss")
                    t_st[st] = P.do("scalar", lambda s, st=st, b2=b2: s.dma_start(
                        out=xln_h[st * 128:(st + 1) * 128, :],
                        in_=xlnb[:, b2, :]),
                        after=[t_ln[st]], sem="xs%d" % b2, n=16)

                t_xT = None
                for kc in range(KC):
                    t_xT = P.do("sync", lambda s, kc=kc: s.dma_start_transpose(
                        out=xT[:, kc, :],
                        in_=xln_h[:, kc * 128:(kc + 1) * 128]),
                        after=[P.now("xs0"), P.now("xs1")], sem="xt", n=16)
                t_xT_all = t_xT
            P.flush(nc, sems)

            vbar = P.barrier_tokens()
            with (
                nc.sbuf_tensor("wv_all", [128, KC, H * D], BF16) as wv_all,
                nc.sbuf_tensor("vout", [128, 1, H * D], BF16) as vout,
                nc.sbuf_tensor("bvb", [128, H * D], BF16) as bvb,
                nc.psum_tensor("ps_v", [128, 8, 512], F32) as ps_v,
            ):
                t_bvb = None
                for ncc in range(NCC):
                    t_bvb = P.do("tensor", lambda t, ncc=ncc: t.matmul(
                        ps_v[:, ncc, :], ones_f[0:1, 0:128],
                        bv_sb[0:1, ncc * 512:(ncc + 1) * 512],
                        start=True, stop=True),
                        after=vbar + [t_ones, t_bv], sem="tt")
                t_bvb = P.do("vector", lambda v: v.tensor_copy(
                    bvb[:, :], ps_v[:, 0:NCC, :].rearrange("p a b -> p (a b)")),
                    after=[t_bvb], sem="vv")
                t_wv = None
                for q4 in range(4):
                    kc0 = q4 * (KC // 4)
                    t_wv = P.do("gpsimd", lambda g, kc0=kc0: g.dma_start(
                        out=wv_all[:, kc0:kc0 + KC // 4, :],
                        in_=wv_r.rearrange("(kc p) n -> p kc n", p=128)
                        [:, kc0:kc0 + KC // 4, :]),
                        after=vbar, sem="gv", n=16)
                t_wv_all = t_wv

                t_evac = [None] * NT
                t_vst = [None] * NT
                for st in range(NT):
                    par = st % 2
                    dep_bank = [t_evac[st - 2]] if st >= 2 else [t_bvb]
                    dep0 = [t_wv_all, t_xT_all] + dep_bank
                    tmm = None
                    for kc in range(KC):
                        for ncc in range(NCC):
                            last = (kc == KC - 1 and ncc == NCC - 1)
                            tmm = P.do("tensor", lambda t, par=par, kc=kc, ncc=ncc, st=st: t.matmul(
                                ps_v[:, par * 4 + ncc, :],
                                xT[:, kc, st * 128:(st + 1) * 128],
                                wv_all[:, kc, ncc * 512:(ncc + 1) * 512],
                                start=(kc == 0), stop=(kc == KC - 1)),
                                after=dep0, sem="tt" if last else None)
                    dep_st = [t_vst[st - 1]] if st >= 1 else []
                    t_evac[st] = P.do("vector", lambda v, par=par: v.tensor_tensor(
                        vout[:, 0, :],
                        ps_v[:, par * 4:par * 4 + NCC, :].rearrange(
                            "p a b -> p (a b)"), bvb[:, :], ALU.add),
                        after=[tmm, t_bvb] + dep_st, sem="vv")
                    t_vst[st] = P.do("sync", lambda s, st=st: s.dma_start(
                        out=v_nat[st * 128:(st + 1) * 128, :],
                        in_=vout[:, 0, :]),
                        after=[t_evac[st]], sem="vn%d" % par, n=16)
                t_vnat_all = t_vst[NT - 1]
            P.flush(nc, sems)

            bbar = P.barrier_tokens()
            with contextlib.ExitStack() as _stk:
                _e = _stk.enter_context
                wqh = _e(nc.sbuf_tensor("wqh", [128, 2, KC, 128], BF16))
                wkh = _e(nc.sbuf_tensor("wkh", [128, 2, KC, 128], BF16))
                v_sb = _e(nc.sbuf_tensor("v_sb", [128, 2, NT, 128], BF16))
                kT = _e(nc.sbuf_tensor("kT", [128, T], BF16))
                qT = _e(nc.sbuf_tensor("qT", [128, TOWN], BF16))
                sq_sb = _e(nc.sbuf_tensor("sq_sb", [128, T], BF16))
                P_sb = _e(nc.sbuf_tensor("P_sb", [128, 4, 512], BF16))
                yT_sb = _e(nc.sbuf_tensor("yT_sb", [128, TOWN], BF16))
                mask_sb = _e(nc.sbuf_tensor("mask_sb", [128, NS * 2 * 128], BF16))
                rec_sb = _e(nc.sbuf_tensor("rec_sb", [1, 512], F32))
                rb_sb = _e(nc.sbuf_tensor("rb_sb", [128, 512], F32))
                mk2 = _e(nc.sbuf_tensor("mk2", [1, 1], F32))
                mq2 = _e(nc.sbuf_tensor("mq2", [1, 1], F32))
                t11 = _e(nc.sbuf_tensor("t11", [1, 1], F32))
                bh = _e(nc.sbuf_tensor("bh", [1, 1], F32))
                negb = _e(nc.sbuf_tensor("negb", [128, 1], F32))
                ps_b = _e(nc.psum_tensor("ps_b", [128, 8, 512], F32))
                t_mask = P.do("sync", lambda s: s.dma_start(
                    out=mask_sb[:, :], in_=masks_in),
                    after=bbar, sem="c1", n=16)

                t_wq = [None] * H
                t_wk = [None] * H
                t_vld = [None] * H

                def prefetch(h, after):
                    par = h % 2
                    for kc in range(KC):
                        t_wq[h] = P.do("sync", lambda s, h=h, par=par, kc=kc: s.dma_start(
                            out=wqh[:, par, kc, :],
                            in_=wq_bf[kc * 128:(kc + 1) * 128,
                                      h * D:(h + 1) * D]),
                            after=after + [t_wqc], sem="wq%d" % par, n=16)
                        t_wk[h] = P.do("sync", lambda s, h=h, par=par, kc=kc: s.dma_start(
                            out=wkh[:, par, kc, :],
                            in_=wk_bf[kc * 128:(kc + 1) * 128,
                                      h * D:(h + 1) * D]),
                            after=after + [t_wkc], sem="wk%d" % par, n=16)
                    for kt in range(NT):
                        t_vld[h] = P.do("sync", lambda s, h=h, par=par, kt=kt: s.dma_start(
                            out=v_sb[:, par, kt, :],
                            in_=v_nat[kt * 128:(kt + 1) * 128,
                                      h * D:(h + 1) * D]),
                            after=after + [P.now("vn0"), P.now("vn1")], sem="vl%d" % par, n=16)

                prefetch(0, bbar)
                t_exp_last = [None] * H
                t_negb_prev = [None]

                t_qproj_done = [None] * H
                t_kproj_done = [None] * H
                t_attn_done = [None] * H
                t_ytst = [None] * H

                for h in range(H):
                    par = h % 2
                    if h + 1 < H:
                        dep = []
                        if h >= 1:
                            dep = [t_qproj_done[h - 1], t_attn_done[h - 1]]
                        prefetch(h + 1, dep)

                    n8 = T // 512
                    nh = n8 // 2
                    if h >= 1:
                        dep_k0a = [t_wk[h], t_xT_all, t_mask,
                                   t_exp_last[h - 1], t_negb_prev[0]]
                        dep_k0b = [t_attn_done[h - 1]]
                    else:
                        dep_k0a = [t_wk[h], t_xT_all, t_mask] + bbar
                        dep_k0b = bbar
                    tmm = None
                    for half, dep_k0 in ((0, dep_k0a), (1, dep_k0b)):
                        for kc in range(KC):
                            for nc8 in range(half * nh, half * nh + nh):
                                last = (kc == KC - 1 and nc8 == n8 - 1)
                                tmm = P.do("tensor", lambda t, par=par, kc=kc, nc8=nc8: t.matmul(
                                    ps_b[:, nc8, :], wkh[:, par, kc, :],
                                    xT[:, kc, nc8 * 512:(nc8 + 1) * 512],
                                    start=(kc == 0), stop=(kc == KC - 1)),
                                    after=dep_k0, sem="tt" if last else None)
                    t_kmm = tmm
                    t_kev = None
                    for nc8 in range(n8):
                        t_kev = P.do("scalar", lambda sc, h=h, nc8=nc8: sc.activation(
                            kT[:, nc8 * 512:(nc8 + 1) * 512], ps_b[:, nc8, :],
                            AF.Identity, bias=bqk_sb[:, H + h:H + h + 1]),
                            after=[t_kmm, t_bqk], sem="ss")
                    t_kproj_done[h] = t_kev

                    tmm = None
                    for kc in range(KC):
                        for nq in range(NQC):
                            last = (kc == KC - 1 and nq == NQC - 1)
                            tmm = P.do("tensor", lambda t, par=par, kc=kc, nq=nq: t.matmul(
                                ps_b[:, 4 + nq, :], wqh[:, par, kc, :],
                                xT[:, kc, nq * 512:(nq + 1) * 512],
                                start=(kc == 0), stop=(kc == KC - 1)),
                                after=[t_wq[h], t_kev], sem="tt" if last else None)
                    t_qmm = tmm
                    t_qev = None
                    for nq in range(NQC):
                        t_qev = P.do("scalar", lambda sc, h=h, nq=nq: sc.activation(
                            qT[:, nq * 512:(nq + 1) * 512], ps_b[:, 4 + nq, :],
                            AF.Identity, bias=bqk_sb[:, h:h + 1]),
                            after=[t_qmm, t_bqk], sem="ss")
                    t_qproj_done[h] = t_qev

                    t_sqk = P.do("vector", lambda v: v.tensor_tensor(
                        sq_sb[:, :], kT[:, :], kT[:, :], ALU.mult),
                        after=[t_kev] + ([t_attn_done[h - 1]] if h >= 1 else []),
                        sem="vv")
                    t_red = None
                    red_by_rg = {2: None, 3: None}
                    for chunk in range(T // 512):
                        rg = 3 if chunk % 2 == 0 else 2
                        dep_m = [t_sqk, t_qmm]
                        if red_by_rg[rg] is not None:
                            dep_m.append(red_by_rg[rg])
                        tm = P.do("tensor", lambda t, chunk=chunk, rg=rg: t.matmul(
                            ps_b[:, rg, :], ones_sq[:, :],
                            sq_sb[:, chunk * 512:(chunk + 1) * 512],
                            start=True, stop=True),
                            after=dep_m + [t_osq], sem="tt")
                        t_r1 = P.do("vector", lambda v, rg=rg: v.tensor_reduce(
                            t11[:, :], ps_b[0:1, rg, :], AX.X, ALU.max),
                            after=[tm] + ([t_red] if t_red is not None else []),
                            sem="vv")
                        t_red = P.do("vector", lambda v, chunk=chunk: (
                            v.tensor_copy(mk2[:, :], t11[:, :]) if chunk == 0
                            else v.tensor_tensor(mk2[:, :], mk2[:, :],
                                                 t11[:, :], ALU.max)),
                            after=[t_r1], sem="vv")
                        red_by_rg[rg] = t_red
                    t_mk2 = t_red
                    t_sqq = P.do("vector", lambda v: v.tensor_tensor(
                        sq_sb[:, 0:TOWN], qT[:, :], qT[:, :], ALU.mult),
                        after=[t_qev, t_mk2], sem="vv")
                    t_red = None
                    for chunk in range(TOWN // 512):
                        rg = 3 if chunk % 2 == 0 else 2
                        dep_m = [t_sqq]
                        dep_m.append(red_by_rg[rg] if red_by_rg[rg] is not None
                                     else t_mk2)
                        tm = P.do("tensor", lambda t, chunk=chunk, rg=rg: t.matmul(
                            ps_b[:, rg, :], ones_sq[:, :],
                            sq_sb[:, chunk * 512:(chunk + 1) * 512],
                            start=True, stop=True),
                            after=dep_m + [t_osq, t_mk2], sem="tt")
                        t_r1 = P.do("vector", lambda v, rg=rg: v.tensor_reduce(
                            t11[:, :], ps_b[0:1, rg, :], AX.X, ALU.max),
                            after=[tm] + ([t_red] if t_red is not None else []),
                            sem="vv")
                        t_red = P.do("vector", lambda v, chunk=chunk: (
                            v.tensor_copy(mq2[:, :], t11[:, :]) if chunk == 0
                            else v.tensor_tensor(mq2[:, :], mq2[:, :],
                                                 t11[:, :], ALU.max)),
                            after=[t_r1], sem="vv")
                        red_by_rg[rg] = t_red
                    t_mq2 = t_red
                    t_prod = P.do("vector", lambda v: v.tensor_tensor(
                        t11[:, :], mq2[:, :], mk2[:, :], ALU.mult),
                        after=[t_mq2], sem="vv")
                    t_bh = P.do("scalar", lambda sc: sc.activation(
                        bh[:, :], t11[:, :], AF.Sqrt, scale=1.0 / D),
                        after=[t_prod], sem="ss")
                    t_nbb = P.do("tensor", lambda t: t.matmul(
                        ps_b[:, 3, 0:1], negones_f[:, :], bh[:, :],
                        start=True, stop=True),
                        after=[t_bh, t_cst, t_mq2], sem="tt")
                    t_negb = P.do("vector", lambda v: v.tensor_copy(
                        negb[:, :], ps_b[:, 3, 0:1]), after=[t_nbb], sem="vv")
                    t_negb_prev[0] = t_negb

                    t_exp = {}
                    t_den = {}
                    t_rbc = {}
                    t_yev = [None] * NQC
                    LAG = 3

                    def emit_S(qc, vi, kts, nk):
                        kt = kts[vi]
                        i2 = vi % 4
                        slot = kt if kt < NS else kt - NS
                        qoff = max(0, (slot - 4 * qc)) * 128
                        masked = (4 * qc <= slot <= 4 * qc + 3)
                        if vi >= 4:
                            dep_s = [t_exp[(qc, vi - 4)]]
                        elif qc >= 1:
                            dep_s = [t_exp[(qc - 1, (8 * qc) - 4 + vi)],
                                     t_negb]
                        else:
                            dep_s = [t_negb, t_qev, t_kev]
                        if qc == 0 and i2 == 3:
                            dep_s = dep_s + [t_negb]
                        t_s = P.do("tensor", lambda t, i2=i2, kt=kt, qoff=qoff, qc=qc: t.matmul(
                            ps_b[:, i2, qoff:512],
                            kT[:, kt * 128:(kt + 1) * 128],
                            qT[:, qc * 512 + qoff:(qc + 1) * 512],
                            start=True, stop=True),
                            after=dep_s, sem="tt")
                        dep_e = [t_s, t_negb]
                        if vi >= 4:
                            dep_e.append(t_den[(qc, vi - 4)])
                        elif qc >= 1:
                            dep_e.append(t_den[(qc - 1, (8 * qc) - 4 + vi)])
                        t_e = P.do("scalar", lambda sc, i2=i2, qoff=qoff: sc.activation(
                            P_sb[:, i2, qoff:512], ps_b[:, i2, qoff:512],
                            AF.Exp, bias=negb[:, :], scale=SCALE),
                            after=dep_e, sem="ss")
                        t_pr = t_e
                        if masked:
                            jm = 0 if kt < NS else 1
                            moff = (slot * 2 + jm) * 128
                            t_pr = P.do("vector", lambda v, i2=i2, qoff=qoff, moff=moff: v.tensor_tensor(
                                P_sb[:, i2, qoff:qoff + 128],
                                P_sb[:, i2, qoff:qoff + 128],
                                mask_sb[:, moff:moff + 128], ALU.mult),
                                after=[t_e, t_mask], sem="vv")
                        t_exp[(qc, vi)] = t_pr

                    def emit_PV(qc, vi, kts, nk, h, par):
                        qcp = qc % 2
                        kt = kts[vi]
                        i2 = vi % 4
                        slot = kt if kt < NS else kt - NS
                        qoff = max(0, (slot - 4 * qc)) * 128
                        first = (vi == 0)
                        is_last = (vi == nk - 1)
                        dep_pv = [t_exp[(qc, vi)], t_vld[h], t_osq]
                        if first:
                            dep_pv.append(t_qev)
                            if qc >= 2:
                                dep_pv.append(t_yev[qc - 2])
                                dep_pv.append(t_rbc.get(qc - 2))
                            if qc in (0, 1) and h >= 1:
                                dep_pv.append(t_attn_done[h - 1])
                        P.do("tensor", lambda t, par=par, kt=kt, qcp=qcp, qoff=qoff, first=first, is_last=is_last, i2=i2: t.matmul(
                            ps_b[:, 4 + qcp, qoff:512],
                            v_sb[:, par, kt, :], P_sb[:, i2, qoff:512],
                            start=first, stop=is_last),
                            after=dep_pv, sem=None)
                        t_den[(qc, vi)] = P.do("tensor", lambda t, qcp=qcp, qoff=qoff, first=first, is_last=is_last, i2=i2: t.matmul(
                            ps_b[:, 6 + qcp, qoff:512], ones_sq[:, :],
                            P_sb[:, i2, qoff:512],
                            start=first, stop=is_last),
                            after=dep_pv, sem="tt")

                    for qc in range(NQC):
                        qcp = qc % 2
                        kts = list(range(0, 4 * qc + 4)) + \
                              list(range(NS, NS + 4 * qc + 4))
                        nk = len(kts)
                        for vi in range(nk):
                            emit_S(qc, vi, kts, nk)
                            if vi >= LAG:
                                emit_PV(qc, vi - LAG, kts, nk, h, par)
                        for vi in range(nk - LAG, nk):
                            emit_PV(qc, vi, kts, nk, h, par)
                        t_dlast = t_den[(qc, nk - 1)]
                        dep_rec = [t_dlast]
                        if qc >= 2:
                            dep_rec.append(t_rbc[qc - 2])
                        t_rec = P.do("vector", lambda v, qcp=qcp: v.reciprocal(
                            rb_sb[:, :], ps_b[:, 6 + qcp, :]),
                            after=dep_rec, sem="vv")
                        dep_ye = [t_rec, t_dlast]
                        if qc == 0 and h >= 1:
                            dep_ye.append(t_ytst[h - 1])
                        t_yev[qc] = P.do("vector", lambda v, qc=qc, qcp=qcp: v.tensor_tensor(
                            yT_sb[:, qc * 512:(qc + 1) * 512],
                            ps_b[:, 4 + qcp, :], rb_sb[:, :],
                            ALU.mult), after=dep_ye, sem="vv")
                        t_rbc[qc] = t_yev[qc]
                    t_attn_done[h] = t_yev[NQC - 1]
                    t_exp_last[h] = t_exp[(NQC - 1, 8 * NQC - 1)]
                    t_ytst[h] = P.do("sync", lambda s, h=h: s.dma_start(
                        out=yt_hb[h, :, :], in_=yT_sb[:, :]),
                        after=[t for t in t_yev], sem="yt", n=16)
                t_yt_all = P.now("yt")
            P.flush(nc, sems)

        cbar = P.barrier_tokens()
        with contextlib.ExitStack() as _stk2:
            _e2 = _stk2.enter_context
            big = _e2(nc.sbuf_tensor("big", [128, KC, C], BF16))
            x1b = _e2(nc.sbuf_tensor("x1b", [128, 2, C], F32))
            x1c = _e2(nc.sbuf_tensor("x1c", [128, 2, C], F32))
            wsc_sb = _e2(nc.sbuf_tensor("wsc_sb", [128, C], F32))
            fb_sb = _e2(nc.sbuf_tensor("fb_sb", [128, C], F32))
            bo_sb = _e2(nc.sbuf_tensor("bo_sb", [1, C], F32))
            bob = _e2(nc.sbuf_tensor("bob", [128, C], F32))
            ones_f2 = _e2(nc.sbuf_tensor("ones_f2", [1, 128], F32))
            yTj = _e2(nc.sbuf_tensor("yTj", [128, 2, H, 128], BF16))
            qTj = _e2(nc.sbuf_tensor("qTj", [128, 2, KC, 256], BF16))
            xin2 = _e2(nc.sbuf_tensor("xin2", [128, 2, C], F32))
            att2 = _e2(nc.sbuf_tensor("att2", [128, 2, C], F32))
            wrk = _e2(nc.sbuf_tensor("wrk", [128, C], F32))
            qnat = _e2(nc.sbuf_tensor("qnat", [128, 2, C], BF16))
            bn6b = _e2(nc.sbuf_tensor("bn6b", [128, 2, NCC, 6], F32))
            mvb = _e2(nc.sbuf_tensor("mvb", [128, 2, 2], F32))
            d_ve = _e2(nc.sbuf_tensor("d_ve", [128, 2], F32))
            d_sd = _e2(nc.sbuf_tensor("d_sd", [128, 2], F32))
            d_rs = _e2(nc.sbuf_tensor("d_rs", [128, 2], F32))
            d_nm = _e2(nc.sbuf_tensor("d_nm", [128, 2], F32))
            d_am = _e2(nc.sbuf_tensor("d_am", [128, 1], F32))
            d_sc = _e2(nc.sbuf_tensor("d_sc", [128, 1], F32))
            d_rc = _e2(nc.sbuf_tensor("d_rc", [128, 2], F32))
            insc = _e2(nc.sbuf_tensor("insc", [128, NS], F32))
            c127 = _e2(nc.sbuf_tensor("c127", [128, 1], F32))
            c1285 = _e2(nc.sbuf_tensor("c1285", [128, 1], F32))
            r127 = _e2(nc.sbuf_tensor("r127", [128, 1], F32))
            qi32 = _e2(nc.sbuf_tensor("qi32", [128, C], mybir.dt.int32))
            ps_c = _e2(nc.psum_tensor("ps_c", [128, 8, 512], F32))
            t_wsc = P.do("sync", lambda s: s.dma_start(out=wsc_sb[:, :],
                                                       in_=wsc_b),
                         after=cbar, sem="c2", n=16)
            t_fb = P.do("sync", lambda s: s.dma_start(out=fb_sb[:, :],
                                                      in_=fbias_b),
                        after=cbar, sem="c2", n=16)
            t_bo = P.do("sync", lambda s: s.dma_start(out=bo_sb[:, :],
                                                      in_=bo_row),
                        after=cbar, sem="c2", n=16)
            t_wsc = t_bo
            t_fb = t_bo
            t_of2 = P.do("vector", lambda v: v.memset(ones_f2[:, :], 1.0),
                         after=cbar, sem="vv")
            t_bob = None
            t_c127 = P.do("vector", lambda v: v.memset(c127[:, :], 127.0),
                          after=cbar, sem="vv")
            t_c1285 = P.do("vector", lambda v: v.memset(c1285[:, :], 128.5),
                           after=cbar, sem="vv")
            t_r127 = P.do("vector", lambda v: v.reciprocal(r127[:, :],
                                                           c127[:, :]),
                          after=[t_c127], sem="vv")
            t_wo = None
            for kc in range(KC):
                t_wo = P.do("sync", lambda s, kc=kc: s.dma_start(
                    out=big[:, kc, :],
                    in_=wo_bf[kc * 128:(kc + 1) * 128, :]),
                    after=cbar + [t_woc], sem="wo", n=16)
            t_wo_all = t_wo

            stats_state = {"n": 0, "user": [None, None]}

            def stats(tok_in, src):
                pp = stats_state["n"] % 2
                stats_state["n"] += 1
                war = [stats_state["user"][pp]] if stats_state["user"][pp] else []
                dep0 = [tok_in] + war
                prev = None
                for g in range(NCC):
                    prev = P.do("vector", lambda v, pp=pp, g=g: v.bn_stats(
                        bn6b[:, pp, g, :], src[:, g * 512:(g + 1) * 512]),
                        after=dep0 + ([prev] if prev else []), sem="vv")
                t_mv = P.do("vector", lambda v, pp=pp: v.bn_aggr(
                    mvb[:, pp, :], bn6b[:, pp, :, :]),
                            after=[prev] + war, sem="vv")
                t_ve = P.do("vector", lambda v, pp=pp: v.tensor_scalar(
                    d_ve[:, pp:pp + 1], mvb[:, pp, 1:2], LN_EPS, None,
                    ALU.add), after=[t_mv] + war, sem="vv")
                t_sq = P.do("scalar", lambda sc, pp=pp: sc.activation(
                    d_sd[:, pp:pp + 1], d_ve[:, pp:pp + 1], AF.Sqrt),
                    after=[t_ve] + war, sem="ss")
                t_rs = P.do("vector", lambda v, pp=pp: v.reciprocal(
                    d_rs[:, pp:pp + 1], d_sd[:, pp:pp + 1]),
                    after=[t_sq] + war, sem="vv")
                t_nm = P.do("vector", lambda v, pp=pp: v.tensor_scalar(
                    d_nm[:, pp:pp + 1], mvb[:, pp, 0:1], d_rs[:, pp:pp + 1],
                    -1.0, ALU.mult, ALU.mult), after=[t_rs, t_mv] + war,
                    sem="vv")
                return t_nm, pp

            t_x1 = [None] * NS
            t_q8st = [None] * NS
            t_omm = [None] * NS
            t_tt_prev = [None] * NS
            t_x1st = [None] * NS
            t_oev = [None] * NS
            t_lnuser = [None] * NS
            t_ommtok = [None] * NS
            t_is_last = [None]
            t_xbl = [None] * NS
            t_xll = [None] * NS

            def c1_head(j):
                jp = j % 2
                dep_y = [t_yt_all]
                if j >= 2:
                    dep_y.append(t_omm[j - 2])
                t_yl = None
                for hq in range(max(1, H // 4)):
                    hn = min(4, H)
                    t_yl = P.do("sync", lambda s, j=j, jp=jp, hq=hq, hn=hn: s.dma_start(
                        out=yTj[:, jp, hq * hn:(hq + 1) * hn, :],
                        in_=yt_hb.rearrange("h p q -> p h q")
                        [:, hq * hn:(hq + 1) * hn, j * 128:(j + 1) * 128]),
                        after=dep_y, sem="yj%d" % jp, n=16)
                dep_x = cbar if j < 2 else [t_x1[j - 2]]
                t_xll[j] = P.do("sync", lambda s, j=j, jp=jp: s.dma_start(
                    out=xin2[:, jp, :],
                    in_=x_seq[j * 128:(j + 1) * 128, :]),
                    after=dep_x, sem="x2%s" % ("a" if jp == 0 else "b"), n=16)
                dep0 = [t_wo_all, t_yl, t_bo, t_of2]
                if j >= 2:
                    dep0.append(t_oev[j - 2])
                tmm = None
                for ncc in range(NCC):
                    tmm = P.do("tensor", lambda t, jp=jp, ncc=ncc: t.matmul(
                        ps_c[:, jp * 4 + ncc, :], ones_f2[:, :],
                        bo_sb[0:1, ncc * 512:(ncc + 1) * 512],
                        start=True, stop=False),
                        after=dep0, sem=None)
                for hc in range(H):
                    for ncc in range(NCC):
                        last = (hc == H - 1 and ncc == NCC - 1)
                        tmm = P.do("tensor", lambda t, jp=jp, hc=hc, ncc=ncc: t.matmul(
                            ps_c[:, jp * 4 + ncc, :], yTj[:, jp, hc, :],
                            big[:, hc, ncc * 512:(ncc + 1) * 512],
                            start=False, stop=(hc == H - 1)),
                            after=dep0, sem="tt" if last else None)
                t_omm[j] = tmm
                t_ommtok[j] = tmm

            def c1_tail(j):
                jp = j % 2
                psrc = ps_c[:, jp * 4:jp * 4 + NCC, :].rearrange(
                    "p a b -> p (a b)")
                t_nm, pp = stats(t_ommtok[j], psrc)
                t_ln2 = P.do("scalar", lambda sc, jp=jp, pp=pp, psrc=psrc: sc.activation(
                    wrk[:, :], psrc, AF.Identity,
                    bias=d_nm[:, pp:pp + 1], scale=d_rs[:, pp:pp + 1]),
                    after=[t_nm, t_ommtok[j]], sem="ss")
                t_oev[j] = t_ln2
                t_lnuser[j] = t_ln2
                stats_state["user"][pp] = t_ln2
                dep_x1 = [t_ln2, t_xll[j]]
                if j >= 2:
                    dep_x1.append(t_x1st[j - 2])
                t_x1[j] = P.do("vector", lambda v, j=j, jp=jp: v.tensor_tensor(
                    x1b[:, jp, :], wrk[:, :], xin2[:, jp, :], ALU.add),
                    after=dep_x1 + [t_ln2], sem="vv")
                t_x1st[j] = P.do("sync", lambda s, j=j, jp=jp: s.dma_start(
                    out=x1_h[j * 128:(j + 1) * 128, :], in_=x1b[:, jp, :]),
                    after=[t_x1[j]], sem="x1s%d" % jp, n=16)

                t_nm, pp = stats(t_x1[j], x1b[:, jp, :])
                t_ln3 = P.do("scalar", lambda sc, jp=jp, pp=pp: sc.activation(
                    wrk[:, :], x1b[:, jp, :], AF.Identity,
                    bias=d_nm[:, pp:pp + 1], scale=d_rs[:, pp:pp + 1]),
                    after=[t_nm, t_x1[j]], sem="ss")
                stats_state["user"][pp] = t_ln3
                t_am = P.do("vector", lambda v: v.tensor_reduce(
                    d_am[:, :], wrk[:, :], AX.X, ALU.max,
                    apply_absolute_value=True), after=[t_ln3], sem="vv")
                t_sc = P.do("vector", lambda v: v.tensor_scalar(
                    d_sc[:, :], d_am[:, :], Q_EPS, r127[:, :], ALU.max,
                    ALU.mult), after=[t_am, t_r127], sem="vv")
                t_is = P.do("vector", lambda v, j=j: v.tensor_copy(
                    insc[:, j:j + 1], d_sc[:, :]), after=[t_sc], sem="vv")
                t_is_last[0] = t_is
                dep_rc = [t_sc]
                if j >= 2:
                    dep_rc.append(t_tt_prev[j - 2])
                t_rc = P.do("vector", lambda v, jp=jp: v.reciprocal(
                    d_rc[:, jp:jp + 1], d_sc[:, :]), after=dep_rc, sem="vv")
                t_t = P.do("scalar", lambda sc, jp=jp: sc.activation(
                    att2[:, jp, :], wrk[:, :], AF.Identity,
                    bias=c1285[:, :], scale=d_rc[:, jp:jp + 1]),
                    after=[t_rc, t_ln3, t_c1285], sem="ss")
                t_tt_prev[j] = t_t
                t_qi = P.do("vector", lambda v, jp=jp: v.tensor_copy(
                    qi32[:, :], att2[:, jp, :]), after=[t_t], sem="vv")
                dep_q = [t_qi]
                if j >= 2:
                    dep_q.append(t_q8st[j - 2])
                t_qn = P.do("vector", lambda v, jp=jp: v.tensor_scalar(
                    qnat[:, jp, :], qi32[:, :], 128.0, None, ALU.subtract),
                    after=dep_q, sem="vv")
                t_lnuser[j] = t_qn
                stats_state["war"] = t_qn
                t_q8st[j] = P.do("sync", lambda s, j=j, jp=jp: s.dma_start(
                    out=q8_h[j * 128:(j + 1) * 128, :], in_=qnat[:, jp, :]),
                    after=[t_qn], sem="q8%d" % jp, n=16)

            for j in range(NS):
                c1_head(j)
                if j >= 1:
                    c1_tail(j - 1)
            c1_tail(NS - 1)
            t_q8_all = P.now("q80")
            t_q8_all2 = P.now("q81")

            bar2 = P.barrier_tokens() + [t_w8c]
            t_wT = None
            for kc in range(KC):
                t_wT = P.do("scalar", lambda s, kc=kc: s.dma_start_transpose(
                    out=big[:, kc, :],
                    in_=w8_bf[:, kc * 128:(kc + 1) * 128]),
                    after=bar2, sem="wt", n=16)
            t_wT_all = t_wT

            t_fmm = [None] * NS
            t_ffnmm = [None] * NS
            t_out = [None] * NS
            t_resid = [None] * NS
            t_ql_pair = [None]

            def c2_head(j):
                jp = j % 2
                if j % 2 == 0:
                    sl = (j // 2) % 2
                    dep_ql = [t_q8st[j], t_q8st[j + 1]]
                    if j >= 4:
                        dep_ql.append(t_ffnmm[j - 3])
                    t_ql = None
                    for kc in range(KC):
                        t_ql = P.do("scalar", lambda s, j=j, sl=sl, kc=kc: s.dma_start_transpose(
                            out=qTj[:, sl, kc, :],
                            in_=q8_h[j * 128:(j + 2) * 128,
                                     kc * 128:(kc + 1) * 128]),
                            after=dep_ql, sem="qj%d" % sl, n=16)
                    t_ql_pair[0] = t_ql
                t_ql = t_ql_pair[0]
                dep_xb = [t_x1st[j]]
                if j >= 2:
                    dep_xb.append(t_resid[j - 2])
                t_xbl[j] = P.do("sync", lambda s, j=j, jp=jp: s.dma_start(
                    out=x1c[:, jp, :], in_=x1_h[j * 128:(j + 1) * 128, :]),
                    after=dep_xb, sem="xc%d" % jp, n=16)
                dep0 = [t_ql, t_wT_all]
                if j >= 2:
                    dep0.append(t_fmm[j - 2])
                tmm = None
                sl2, dd = (j // 2) % 2, j % 2
                for kc in range(KC):
                    for ncc in range(NCC):
                        last = (kc == KC - 1 and ncc == NCC - 1)
                        tmm = P.do("tensor", lambda t, jp=jp, kc=kc, ncc=ncc, sl2=sl2, dd=dd: t.matmul(
                            ps_c[:, jp * 4 + ncc, :],
                            qTj[:, sl2, kc, dd * 128:(dd + 1) * 128],
                            big[:, kc, ncc * 512:(ncc + 1) * 512],
                            start=(kc == 0), stop=(kc == KC - 1)),
                            after=dep0, sem="tt" if last else None)
                t_ffnmm[j] = tmm

            def c2_tail(j):
                jp = j % 2
                tmm = t_ffnmm[j]
                dep_s1 = [tmm, t_is_last[0], t_wsc]
                if j >= 2:
                    dep_s1.append(t_resid[j - 2])
                    dep_s1.append(t_out[j - 2])
                t_s1 = P.do("scalar", lambda sc, j=j, jp=jp: sc.activation(
                    att2[:, jp, :], ps_c[:, jp * 4:jp * 4 + NCC, :].rearrange(
                        "p a b -> p (a b)"), AF.Identity,
                    bias=0.0, scale=insc[:, j:j + 1]),
                    after=dep_s1, sem="ss")
                t_fmm[j] = t_s1
                t_s2 = P.do("gpsimd", lambda g, jp=jp: g.tensor_tensor(
                    att2[:, jp, :], att2[:, jp, :], wsc_sb[:, :], ALU.mult),
                    after=[t_s1, t_wsc], sem="gg")
                t_s3 = P.do("gpsimd", lambda g, jp=jp: g.tensor_tensor(
                    att2[:, jp, :], att2[:, jp, :], fb_sb[:, :], ALU.add),
                    after=[t_s2, t_fb], sem="gg")
                t_nm, pp = stats(t_s3, att2[:, jp, :])
                t_ln4 = P.do("scalar", lambda sc, jp=jp, pp=pp: sc.activation(
                    wrk[:, :], att2[:, jp, :], AF.Identity,
                    bias=d_nm[:, pp:pp + 1], scale=d_rs[:, pp:pp + 1]),
                    after=[t_nm, t_s3], sem="ss")
                t_resid[j] = P.do("vector", lambda v, jp=jp: v.tensor_tensor(
                    att2[:, jp, :], wrk[:, :], x1c[:, jp, :], ALU.add),
                    after=[t_ln4, t_xbl[j]], sem="vv")
                stats_state["user"][pp] = t_resid[j]
                t_out[j] = P.do("sync", lambda s, j=j, jp=jp: s.dma_start(
                    out=out_t[j * 128:(j + 1) * 128, :], in_=att2[:, jp, :]),
                    after=[t_resid[j]], sem="out%d" % jp, n=16)

            for j in range(NS):
                c2_head(j)
                if j >= 1:
                    c2_tail(j - 1)
            c2_tail(NS - 1)
            P.do("sync", None, after=[t_out[NS - 2], t_out[NS - 1]])
            P.flush(nc, sems)

    return nc


def make_in_maps(x, Wq, bq, Wk, bk, Wv, bv, Wo, bo, w8, w_scale, fbias,
                 T=4096, C=2048, H=16):
    D = 128
    NS = (T // 128) // 2
    bf = ml_dtypes.bfloat16
    B = x.shape[0]

    wq_r = np.ascontiguousarray(Wq.transpose(1, 0, 2).reshape(C, H * D))
    wk_r = np.ascontiguousarray(Wk.transpose(1, 0, 2).reshape(C, H * D))
    wv_r = np.ascontiguousarray(Wv.transpose(1, 0, 2).reshape(C, H * D))
    wo = np.ascontiguousarray(Wo)
    bqk = np.ascontiguousarray(np.concatenate([bq.T, bk.T], axis=1))
    bv_row = np.ascontiguousarray(bv.reshape(1, H * D))
    bo_row = np.ascontiguousarray(bo.reshape(1, C))
    w8c = np.ascontiguousarray(w8)
    wsc_b = np.ascontiguousarray(
        np.broadcast_to(w_scale.reshape(1, C), (128, C)))
    fb_b = np.ascontiguousarray(
        np.broadcast_to(fbias.reshape(1, C), (128, C)))

    tri = np.tril(np.ones((128, 128), np.float32)).T
    zeros = np.zeros((128, 128), np.float32)
    ones = np.ones((128, 128), np.float32)

    in_maps = []
    for core in range(2 * B):
        b, h = divmod(core, 2)
        perm = [2 * i + h for i in range(NS)] + \
               [2 * i + (1 - h) for i in range(NS)]
        rows = np.concatenate([np.arange(t0 * 128, (t0 + 1) * 128)
                               for t0 in perm])
        x_core = np.ascontiguousarray(x[b][rows])
        m = np.empty((128, NS * 2 * 128), np.float32)
        for i in range(NS):
            m[:, (2 * i) * 128:(2 * i + 1) * 128] = tri
            m[:, (2 * i + 1) * 128:(2 * i + 2) * 128] = \
                zeros if h == 0 else ones
        in_maps.append({
            "x_seq": x_core,
            "wq_r": wq_r, "wk_r": wk_r, "wv_r": wv_r, "wo_in": wo,
            "bqk_cols": bqk, "bv_row": bv_row, "bo_row": bo_row,
            "w8": w8c, "wsc_b": wsc_b, "fbias_b": fb_b,
            "masks_in": m.astype(bf),
        })
    return in_maps


def assemble_out(results, B, T=4096, C=2048):
    NS = (T // 128) // 2
    out = np.empty((B, T, C), np.float32)
    for core in range(2 * B):
        b, h = divmod(core, 2)
        o = np.asarray(results[core]["out"])
        for i in range(NS):
            g = 2 * i + h
            out[b, g * 128:(g + 1) * 128, :] = o[i * 128:(i + 1) * 128, :]
    return out


_NC_CACHE = {}


def kernel(L0_attn_x, L0_attn_Wq, L0_attn_bq, L0_attn_Wk, L0_attn_bk,
           L0_attn_Wv, L0_attn_bv, L0_attn_Wo, L0_attn_bo,
           L0_ffn_weight, L0_ffn_w_scale, L0_ffn_bias):
    x = np.asarray(L0_attn_x)
    B = x.shape[0]
    in_maps = make_in_maps(
        x, np.asarray(L0_attn_Wq), np.asarray(L0_attn_bq),
        np.asarray(L0_attn_Wk), np.asarray(L0_attn_bk),
        np.asarray(L0_attn_Wv), np.asarray(L0_attn_bv),
        np.asarray(L0_attn_Wo), np.asarray(L0_attn_bo),
        np.asarray(L0_ffn_weight), np.asarray(L0_ffn_w_scale),
        np.asarray(L0_ffn_bias))
    if "nc" not in _NC_CACHE:
        _NC_CACHE["nc"] = build_nc()
    res = run_bass_kernel_spmd(_NC_CACHE["nc"], in_maps,
                               core_ids=list(range(2 * B)))
    return assemble_out(res.results, B)


# version 32 (final)
# speedup vs baseline: 1.0918x; 1.0918x over previous
import contextlib
import numpy as np
import ml_dtypes
import concourse.bass as bass
import concourse.mybir as mybir
from concourse.bass_utils import run_bass_kernel_spmd

F32 = mybir.dt.float32
BF16 = mybir.dt.bfloat16
I8 = mybir.dt.int8
ALU = mybir.AluOpType
AF = mybir.ActivationFunctionType
AX = mybir.AxisListType

LN_EPS = 1e-5
Q_EPS = 1e-7


class Prog:
    ENGINES = ("sync", "tensor", "vector", "scalar", "gpsimd")

    def __init__(self, sem_names):
        self.ev = {e: [] for e in self.ENGINES}
        self.cnt = {k: 0 for k in sem_names}
        self.hw = {e: {} for e in self.ENGINES}

    def do(self, eng, fn, after=(), sem=None, n=1):
        waits = {}
        for tok in after:
            if tok is None:
                continue
            s, v = tok
            if waits.get(s, 0) < v:
                waits[s] = v
        tok = None
        if sem is not None:
            self.cnt[sem] += n
            tok = (sem, self.cnt[sem])
        self.ev[eng].append((fn, waits, sem, n))
        return tok

    def now(self, sem):
        return (sem, self.cnt[sem])

    def barrier_tokens(self):
        return [(s, v) for s, v in self.cnt.items() if v > 0]

    def flush(self, nc, sems):
        with nc.Block() as block:
            for eng_name in self.ENGINES:
                events = self.ev[eng_name]
                if not events:
                    continue

                def make(eng_name, events):
                    def body(eng):
                        hww = self.hw[eng_name]
                        for fn, waits, sem, n in events:
                            for s, v in waits.items():
                                if hww.get(s, 0) < v:
                                    eng.wait_ge(sems[s], v)
                                    hww[s] = v
                            if fn is None:
                                continue
                            instr = fn(eng)
                            if sem is not None:
                                instr.then_inc(sems[sem], n)
                    return body

                getattr(block, eng_name)(make(eng_name, events))
        self.ev = {e: [] for e in self.ENGINES}


def build_nc(T=4096, C=2048, H=16, detect_races=False):
    D = 128
    NT = T // 128
    KC = C // 128
    NS = NT // 2
    TOWN = NS * 128
    NQC = TOWN // 512
    NCC = C // 512
    SCALE = float(D) ** -0.5

    nc = bass.Bass("TRN2", target_bir_lowering=False,
                   detect_race_conditions=detect_races,
                   dynamic_dma_scratch_size=8192)

    x_seq = nc.dram_tensor("x_seq", [T, C], F32, kind="ExternalInput").ap()
    wq_r = nc.dram_tensor("wq_r", [C, H * D], F32, kind="ExternalInput").ap()
    wk_r = nc.dram_tensor("wk_r", [C, H * D], F32, kind="ExternalInput").ap()
    wv_r = nc.dram_tensor("wv_r", [C, H * D], F32, kind="ExternalInput").ap()
    wo_in = nc.dram_tensor("wo_in", [C, C], F32, kind="ExternalInput").ap()
    bqk_cols = nc.dram_tensor("bqk_cols", [D, 2 * H], F32, kind="ExternalInput").ap()
    bv_row = nc.dram_tensor("bv_row", [1, H * D], F32, kind="ExternalInput").ap()
    bo_row = nc.dram_tensor("bo_row", [1, C], F32, kind="ExternalInput").ap()
    w8 = nc.dram_tensor("w8", [C, C], I8, kind="ExternalInput").ap()
    wsc_b = nc.dram_tensor("wsc_b", [128, C], F32, kind="ExternalInput").ap()
    fbias_b = nc.dram_tensor("fbias_b", [128, C], F32, kind="ExternalInput").ap()
    masks_in = nc.dram_tensor("masks_in", [128, NS * 2 * 128], BF16,
                              kind="ExternalInput").ap()
    out_t = nc.dram_tensor("out", [TOWN, C], F32, kind="ExternalOutput").ap()

    xln_h = nc.dram_tensor("xln_h", [T, C], BF16).ap()
    wq_bf = nc.dram_tensor("wq_bf", [C, H * D], BF16).ap()
    wk_bf = nc.dram_tensor("wk_bf", [C, H * D], BF16).ap()
    wo_bf = nc.dram_tensor("wo_bf", [C, C], BF16).ap()
    w8_bf = nc.dram_tensor("w8_bf", [C, C], BF16).ap()
    v_nat = nc.dram_tensor("v_nat", [T, H * D], BF16).ap()
    yt_hb = nc.dram_tensor("yt_hb", [H, 128, TOWN], BF16).ap()
    q8_h = nc.dram_tensor("q8_h", [TOWN, C], BF16).ap()
    x1_h = nc.dram_tensor("x1_h", [TOWN, C], F32).ap()

    SEM_NAMES = ["tt", "vv", "ss", "gg", "gw", "gv",
                 "c1", "c2", "xi0", "xi1", "xi2", "xi3", "xs0", "xs1", "xt", "vn0", "vn1",
                 "wq0", "wq1", "wk0", "wk1", "vl0", "vl1", "yt",
                 "wo", "yj0", "yj1", "x2a", "x2b", "x1s0", "x1s1", "q80", "q81", "wt",
                 "qj0", "qj1", "xc0", "xc1", "out0", "out1"]
    P = Prog(SEM_NAMES)

    with contextlib.ExitStack() as _semstk:
        sems = {k: _semstk.enter_context(nc.semaphore(k)) for k in SEM_NAMES}

        with (
            nc.sbuf_tensor("xT", [128, KC, T], BF16) as xT,
            nc.sbuf_tensor("ones_f", [1, 128], F32) as ones_f,
            nc.sbuf_tensor("ones_cb", [128, 1], BF16) as ones_cb,
            nc.sbuf_tensor("ones_sq", [128, 128], BF16) as ones_sq,
            nc.sbuf_tensor("negones_f", [1, 128], F32) as negones_f,
            nc.sbuf_tensor("bqk_sb", [128, 2 * H], F32) as bqk_sb,
            nc.sbuf_tensor("bv_sb", [1, H * D], F32) as bv_sb,
        ):
            t_ones = P.do("vector", lambda v: v.memset(ones_f[:, :], 1.0),
                          sem="vv")
            t_ocb = P.do("vector", lambda v: v.memset(ones_cb[:, :], 1.0),
                         sem="vv")
            t_osq = P.do("vector", lambda v: v.memset(ones_sq[:, :], 1.0),
                         sem="vv")
            t_cst = P.do("vector", lambda v: v.memset(negones_f[:, :], -1.0),
                         sem="vv")
            t_bqk = P.do("sync", lambda s: s.dma_start(out=bqk_sb[:, :],
                                                       in_=bqk_cols),
                         sem="c1", n=16)
            t_bv = P.do("sync", lambda s: s.dma_start(out=bv_sb[:, :],
                                                      in_=bv_row),
                        sem="c1", n=16)
            t_bqk = t_bv

            t_wqc = P.do("gpsimd", lambda g: g.dma_start(out=wq_bf, in_=wq_r),
                         sem="gw", n=16)
            t_wkc = P.do("gpsimd", lambda g: g.dma_start(out=wk_bf, in_=wk_r),
                         sem="gw", n=16)
            t_woc = P.do("gpsimd", lambda g: g.dma_start(out=wo_bf, in_=wo_in),
                         sem="gw", n=16)
            t_w8c = P.do("gpsimd", lambda g: g.dma_start(out=w8_bf, in_=w8),
                         sem="gw", n=16)

            with (
                nc.sbuf_tensor("xin", [128, 4, C], F32) as xin,
                nc.sbuf_tensor("xlnb", [128, 2, C], BF16) as xlnb,
                nc.sbuf_tensor("bn6", [128, 2, NCC, 6], F32) as bn6,
                nc.sbuf_tensor("mv", [128, 2, 2], F32) as mv,
                nc.sbuf_tensor("c_ve", [128, 2], F32) as c_ve,
                nc.sbuf_tensor("c_sd", [128, 2], F32) as c_sd,
                nc.sbuf_tensor("c_rs", [128, 2], F32) as c_rs,
                nc.sbuf_tensor("c_nm", [128, 2], F32) as c_nm,
            ):
                t_ld = [None] * NT
                t_st = [None] * NT
                t_ln = [None] * NT
                t_rs_prev = None
                for st in range(NT):
                    b3, b2 = st % 4, st % 2
                    dep = [t_ln[st - 4]] if st >= 4 else []
                    t_ld[st] = P.do("sync",
                                    lambda s, st=st, b3=b3: s.dma_start(
                        out=xin[:, b3, :],
                        in_=x_seq[st * 128:(st + 1) * 128, :]),
                        after=dep, sem="xi%d" % b3, n=16)
                    prev = t_ld[st]
                    dep_bn = [t_ld[st]]
                    if st >= 2:
                        dep_bn.append(t_ln[st - 2])
                    for g in range(NCC):
                        prev = P.do("vector", lambda v, b3=b3, b2=b2, g=g: v.bn_stats(
                            bn6[:, b2, g, :],
                            xin[:, b3, g * 512:(g + 1) * 512]),
                            after=dep_bn + [prev], sem="vv")
                    t_mv = P.do("vector", lambda v, b2=b2: v.bn_aggr(
                        mv[:, b2, :], bn6[:, b2, :, :]),
                                after=[prev], sem="vv")
                    t_ve = P.do("vector", lambda v, b2=b2: v.tensor_scalar(
                        c_ve[:, b2:b2 + 1], mv[:, b2, 1:2], LN_EPS, None,
                        ALU.add), after=[t_mv], sem="vv")
                    dep_sq = [t_ve]
                    if t_rs_prev is not None:
                        dep_sq.append(t_rs_prev)
                    t_sq = P.do("scalar", lambda sc, b2=b2: sc.activation(
                        c_sd[:, b2:b2 + 1], c_ve[:, b2:b2 + 1], AF.Sqrt),
                        after=dep_sq, sem="ss")
                    t_rs = P.do("vector", lambda v, b2=b2: v.reciprocal(
                        c_rs[:, b2:b2 + 1], c_sd[:, b2:b2 + 1]),
                        after=[t_sq], sem="vv")
                    t_rs_prev = t_rs
                    t_nm = P.do("vector", lambda v, b2=b2: v.tensor_scalar(
                        c_nm[:, b2:b2 + 1], mv[:, b2, 0:1], c_rs[:, b2:b2 + 1],
                        -1.0, ALU.mult, ALU.mult), after=[t_rs, t_mv],
                        sem="vv")
                    dep2 = [t_st[st - 2]] if st >= 2 else []
                    t_ln[st] = P.do("scalar", lambda sc, b3=b3, b2=b2: sc.activation(
                        xlnb[:, b2, :], xin[:, b3, :], AF.Identity,
                        bias=c_nm[:, b2:b2 + 1], scale=c_rs[:, b2:b2 + 1]),
                        after=[t_nm, t_ld[st]] + dep2, sem="ss")
                    t_st[st] = P.do("scalar", lambda s, st=st, b2=b2: s.dma_start(
                        out=xln_h[st * 128:(st + 1) * 128, :],
                        in_=xlnb[:, b2, :]),
                        after=[t_ln[st]], sem="xs%d" % b2, n=16)

                t_xT = None
                for kc in range(KC):
                    t_xT = P.do("scalar", lambda s, kc=kc: s.dma_start_transpose(
                        out=xT[:, kc, :],
                        in_=xln_h[:, kc * 128:(kc + 1) * 128]),
                        after=[P.now("xs0"), P.now("xs1")], sem="xt", n=16)
                t_xT_all = t_xT
            P.flush(nc, sems)

            vbar = P.barrier_tokens()
            with (
                nc.sbuf_tensor("wv_all", [128, KC, H * D], BF16) as wv_all,
                nc.sbuf_tensor("vout", [128, 1, H * D], BF16) as vout,
                nc.sbuf_tensor("bvb", [128, H * D], BF16) as bvb,
                nc.psum_tensor("ps_v", [128, 8, 512], F32) as ps_v,
            ):
                t_bvb = None
                for ncc in range(NCC):
                    t_bvb = P.do("tensor", lambda t, ncc=ncc: t.matmul(
                        ps_v[:, ncc, :], ones_f[0:1, 0:128],
                        bv_sb[0:1, ncc * 512:(ncc + 1) * 512],
                        start=True, stop=True),
                        after=vbar + [t_ones, t_bv], sem="tt")
                t_bvb = P.do("vector", lambda v: v.tensor_copy(
                    bvb[:, :], ps_v[:, 0:NCC, :].rearrange("p a b -> p (a b)")),
                    after=[t_bvb], sem="vv")
                t_wv = None
                for q4 in range(4):
                    kc0 = q4 * (KC // 4)
                    t_wv = P.do("gpsimd", lambda g, kc0=kc0: g.dma_start(
                        out=wv_all[:, kc0:kc0 + KC // 4, :],
                        in_=wv_r.rearrange("(kc p) n -> p kc n", p=128)
                        [:, kc0:kc0 + KC // 4, :]),
                        after=vbar, sem="gv", n=16)
                t_wv_all = t_wv

                t_evac = [None] * NT
                t_vst = [None] * NT
                for st in range(NT):
                    par = st % 2
                    dep_bank = [t_evac[st - 2]] if st >= 2 else [t_bvb]
                    dep0 = [t_wv_all, t_xT_all] + dep_bank
                    tmm = None
                    for kc in range(KC):
                        for ncc in range(NCC):
                            last = (kc == KC - 1 and ncc == NCC - 1)
                            tmm = P.do("tensor", lambda t, par=par, kc=kc, ncc=ncc, st=st: t.matmul(
                                ps_v[:, par * 4 + ncc, :],
                                xT[:, kc, st * 128:(st + 1) * 128],
                                wv_all[:, kc, ncc * 512:(ncc + 1) * 512],
                                start=(kc == 0), stop=(kc == KC - 1)),
                                after=dep0, sem="tt" if last else None)
                    dep_st = [t_vst[st - 1]] if st >= 1 else []
                    t_evac[st] = P.do("vector", lambda v, par=par: v.tensor_tensor(
                        vout[:, 0, :],
                        ps_v[:, par * 4:par * 4 + NCC, :].rearrange(
                            "p a b -> p (a b)"), bvb[:, :], ALU.add),
                        after=[tmm, t_bvb] + dep_st, sem="vv")
                    t_vst[st] = P.do("sync", lambda s, st=st: s.dma_start(
                        out=v_nat[st * 128:(st + 1) * 128, :],
                        in_=vout[:, 0, :]),
                        after=[t_evac[st]], sem="vn%d" % par, n=16)
                t_vnat_all = t_vst[NT - 1]
            P.flush(nc, sems)

            bbar = P.barrier_tokens()
            with contextlib.ExitStack() as _stk:
                _e = _stk.enter_context
                wqh = _e(nc.sbuf_tensor("wqh", [128, 2, KC, 128], BF16))
                wkh = _e(nc.sbuf_tensor("wkh", [128, 2, KC, 128], BF16))
                v_sb = _e(nc.sbuf_tensor("v_sb", [128, 2, NT, 128], BF16))
                kT = _e(nc.sbuf_tensor("kT", [128, T], BF16))
                qT = _e(nc.sbuf_tensor("qT", [128, TOWN], BF16))
                sq_sb = _e(nc.sbuf_tensor("sq_sb", [128, T], BF16))
                P_sb = _e(nc.sbuf_tensor("P_sb", [128, 4, 512], BF16))
                yT_sb = _e(nc.sbuf_tensor("yT_sb", [128, TOWN], BF16))
                mask_sb = _e(nc.sbuf_tensor("mask_sb", [128, NS * 2 * 128], BF16))
                rec_sb = _e(nc.sbuf_tensor("rec_sb", [1, 512], F32))
                rb_sb = _e(nc.sbuf_tensor("rb_sb", [128, 512], F32))
                mk2 = _e(nc.sbuf_tensor("mk2", [1, 1], F32))
                mq2 = _e(nc.sbuf_tensor("mq2", [1, 1], F32))
                t11 = _e(nc.sbuf_tensor("t11", [1, 1], F32))
                bh = _e(nc.sbuf_tensor("bh", [1, 1], F32))
                negb = _e(nc.sbuf_tensor("negb", [128, 1], F32))
                ps_b = _e(nc.psum_tensor("ps_b", [128, 8, 512], F32))
                t_mask = P.do("sync", lambda s: s.dma_start(
                    out=mask_sb[:, :], in_=masks_in),
                    after=bbar, sem="c1", n=16)

                t_wq = [None] * H
                t_wk = [None] * H
                t_vld = [None] * H

                def prefetch(h, after):
                    par = h % 2
                    for kc in range(KC):
                        t_wq[h] = P.do("sync", lambda s, h=h, par=par, kc=kc: s.dma_start(
                            out=wqh[:, par, kc, :],
                            in_=wq_bf[kc * 128:(kc + 1) * 128,
                                      h * D:(h + 1) * D]),
                            after=after + [t_wqc], sem="wq%d" % par, n=16)
                        t_wk[h] = P.do("sync", lambda s, h=h, par=par, kc=kc: s.dma_start(
                            out=wkh[:, par, kc, :],
                            in_=wk_bf[kc * 128:(kc + 1) * 128,
                                      h * D:(h + 1) * D]),
                            after=after + [t_wkc], sem="wk%d" % par, n=16)
                    for kt in range(NT):
                        t_vld[h] = P.do("sync", lambda s, h=h, par=par, kt=kt: s.dma_start(
                            out=v_sb[:, par, kt, :],
                            in_=v_nat[kt * 128:(kt + 1) * 128,
                                      h * D:(h + 1) * D]),
                            after=after + [P.now("vn0"), P.now("vn1")], sem="vl%d" % par, n=16)

                prefetch(0, bbar)
                t_exp_last = [None] * H
                t_negb_prev = [None]

                t_qproj_done = [None] * H
                t_kproj_done = [None] * H
                t_attn_done = [None] * H
                t_ytst = [None] * H

                for h in range(H):
                    par = h % 2
                    if h + 1 < H:
                        dep = []
                        if h >= 1:
                            dep = [t_qproj_done[h - 1], t_attn_done[h - 1]]
                        prefetch(h + 1, dep)

                    n8 = T // 512
                    nh = n8 // 2
                    if h >= 1:
                        dep_k0a = [t_wk[h], t_xT_all, t_mask,
                                   t_exp_last[h - 1], t_negb_prev[0]]
                        dep_k0b = [t_attn_done[h - 1]]
                    else:
                        dep_k0a = [t_wk[h], t_xT_all, t_mask] + bbar
                        dep_k0b = bbar
                    tmm = None
                    for half, dep_k0 in ((0, dep_k0a), (1, dep_k0b)):
                        for kc in range(KC):
                            for nc8 in range(half * nh, half * nh + nh):
                                last = (kc == KC - 1 and nc8 == n8 - 1)
                                tmm = P.do("tensor", lambda t, par=par, kc=kc, nc8=nc8: t.matmul(
                                    ps_b[:, nc8, :], wkh[:, par, kc, :],
                                    xT[:, kc, nc8 * 512:(nc8 + 1) * 512],
                                    start=(kc == 0), stop=(kc == KC - 1)),
                                    after=dep_k0, sem="tt" if last else None)
                    t_kmm = tmm
                    t_kev = None
                    for nc8 in range(n8):
                        t_kev = P.do("scalar", lambda sc, h=h, nc8=nc8: sc.activation(
                            kT[:, nc8 * 512:(nc8 + 1) * 512], ps_b[:, nc8, :],
                            AF.Identity, bias=bqk_sb[:, H + h:H + h + 1]),
                            after=[t_kmm, t_bqk], sem="ss")
                    t_kproj_done[h] = t_kev

                    tmm = None
                    for kc in range(KC):
                        for nq in range(NQC):
                            last = (kc == KC - 1 and nq == NQC - 1)
                            tmm = P.do("tensor", lambda t, par=par, kc=kc, nq=nq: t.matmul(
                                ps_b[:, 4 + nq, :], wqh[:, par, kc, :],
                                xT[:, kc, nq * 512:(nq + 1) * 512],
                                start=(kc == 0), stop=(kc == KC - 1)),
                                after=[t_wq[h], t_kev], sem="tt" if last else None)
                    t_qmm = tmm
                    t_qev = None
                    for nq in range(NQC):
                        t_qev = P.do("scalar", lambda sc, h=h, nq=nq: sc.activation(
                            qT[:, nq * 512:(nq + 1) * 512], ps_b[:, 4 + nq, :],
                            AF.Identity, bias=bqk_sb[:, h:h + 1]),
                            after=[t_qmm, t_bqk], sem="ss")
                    t_qproj_done[h] = t_qev

                    t_sqk = P.do("vector", lambda v: v.tensor_tensor(
                        sq_sb[:, :], kT[:, :], kT[:, :], ALU.mult),
                        after=[t_kev] + ([t_attn_done[h - 1]] if h >= 1 else []),
                        sem="vv")
                    t_red = None
                    red_by_rg = {2: None, 3: None}
                    for chunk in range(T // 512):
                        rg = 3 if chunk % 2 == 0 else 2
                        dep_m = [t_sqk, t_qmm]
                        if red_by_rg[rg] is not None:
                            dep_m.append(red_by_rg[rg])
                        tm = P.do("tensor", lambda t, chunk=chunk, rg=rg: t.matmul(
                            ps_b[:, rg, :], ones_sq[:, :],
                            sq_sb[:, chunk * 512:(chunk + 1) * 512],
                            start=True, stop=True),
                            after=dep_m + [t_osq], sem="tt")
                        t_r1 = P.do("vector", lambda v, rg=rg: v.tensor_reduce(
                            t11[:, :], ps_b[0:1, rg, :], AX.X, ALU.max),
                            after=[tm] + ([t_red] if t_red is not None else []),
                            sem="vv")
                        t_red = P.do("vector", lambda v, chunk=chunk: (
                            v.tensor_copy(mk2[:, :], t11[:, :]) if chunk == 0
                            else v.tensor_tensor(mk2[:, :], mk2[:, :],
                                                 t11[:, :], ALU.max)),
                            after=[t_r1], sem="vv")
                        red_by_rg[rg] = t_red
                    t_mk2 = t_red
                    t_sqq = P.do("vector", lambda v: v.tensor_tensor(
                        sq_sb[:, 0:TOWN], qT[:, :], qT[:, :], ALU.mult),
                        after=[t_qev, t_mk2], sem="vv")
                    t_red = None
                    for chunk in range(TOWN // 512):
                        rg = 3 if chunk % 2 == 0 else 2
                        dep_m = [t_sqq]
                        dep_m.append(red_by_rg[rg] if red_by_rg[rg] is not None
                                     else t_mk2)
                        tm = P.do("tensor", lambda t, chunk=chunk, rg=rg: t.matmul(
                            ps_b[:, rg, :], ones_sq[:, :],
                            sq_sb[:, chunk * 512:(chunk + 1) * 512],
                            start=True, stop=True),
                            after=dep_m + [t_osq, t_mk2], sem="tt")
                        t_r1 = P.do("vector", lambda v, rg=rg: v.tensor_reduce(
                            t11[:, :], ps_b[0:1, rg, :], AX.X, ALU.max),
                            after=[tm] + ([t_red] if t_red is not None else []),
                            sem="vv")
                        t_red = P.do("vector", lambda v, chunk=chunk: (
                            v.tensor_copy(mq2[:, :], t11[:, :]) if chunk == 0
                            else v.tensor_tensor(mq2[:, :], mq2[:, :],
                                                 t11[:, :], ALU.max)),
                            after=[t_r1], sem="vv")
                        red_by_rg[rg] = t_red
                    t_mq2 = t_red
                    t_prod = P.do("vector", lambda v: v.tensor_tensor(
                        t11[:, :], mq2[:, :], mk2[:, :], ALU.mult),
                        after=[t_mq2], sem="vv")
                    t_bh = P.do("scalar", lambda sc: sc.activation(
                        bh[:, :], t11[:, :], AF.Sqrt, scale=1.0 / D),
                        after=[t_prod], sem="ss")
                    t_nbb = P.do("tensor", lambda t: t.matmul(
                        ps_b[:, 3, 0:1], negones_f[:, :], bh[:, :],
                        start=True, stop=True),
                        after=[t_bh, t_cst, t_mq2], sem="tt")
                    t_negb = P.do("vector", lambda v: v.tensor_copy(
                        negb[:, :], ps_b[:, 3, 0:1]), after=[t_nbb], sem="vv")
                    t_negb_prev[0] = t_negb

                    t_exp = {}
                    t_den = {}
                    t_rbc = {}
                    t_yev = [None] * NQC
                    LAG = 3

                    def emit_S(qc, vi, kts, nk):
                        kt = kts[vi]
                        i2 = vi % 4
                        slot = kt if kt < NS else kt - NS
                        qoff = max(0, (slot - 4 * qc)) * 128
                        masked = (4 * qc <= slot <= 4 * qc + 3)
                        if vi >= 4:
                            dep_s = [t_exp[(qc, vi - 4)]]
                        elif qc >= 1:
                            dep_s = [t_exp[(qc - 1, (8 * qc) - 4 + vi)],
                                     t_negb]
                        else:
                            dep_s = [t_negb, t_qev, t_kev]
                        if qc == 0 and i2 == 3:
                            dep_s = dep_s + [t_negb]
                        t_s = P.do("tensor", lambda t, i2=i2, kt=kt, qoff=qoff, qc=qc: t.matmul(
                            ps_b[:, i2, qoff:512],
                            kT[:, kt * 128:(kt + 1) * 128],
                            qT[:, qc * 512 + qoff:(qc + 1) * 512],
                            start=True, stop=True),
                            after=dep_s, sem="tt")
                        dep_e = [t_s, t_negb]
                        if vi >= 4:
                            dep_e.append(t_den[(qc, vi - 4)])
                        elif qc >= 1:
                            dep_e.append(t_den[(qc - 1, (8 * qc) - 4 + vi)])
                        t_e = P.do("scalar", lambda sc, i2=i2, qoff=qoff: sc.activation(
                            P_sb[:, i2, qoff:512], ps_b[:, i2, qoff:512],
                            AF.Exp, bias=negb[:, :], scale=SCALE),
                            after=dep_e, sem="ss")
                        t_pr = t_e
                        if masked:
                            jm = 0 if kt < NS else 1
                            moff = (slot * 2 + jm) * 128
                            t_pr = P.do("vector", lambda v, i2=i2, qoff=qoff, moff=moff: v.tensor_tensor(
                                P_sb[:, i2, qoff:qoff + 128],
                                P_sb[:, i2, qoff:qoff + 128],
                                mask_sb[:, moff:moff + 128], ALU.mult),
                                after=[t_e, t_mask], sem="vv")
                        t_exp[(qc, vi)] = t_pr

                    def emit_PV(qc, vi, kts, nk, h, par):
                        qcp = qc % 2
                        kt = kts[vi]
                        i2 = vi % 4
                        slot = kt if kt < NS else kt - NS
                        qoff = max(0, (slot - 4 * qc)) * 128
                        first = (vi == 0)
                        is_last = (vi == nk - 1)
                        dep_pv = [t_exp[(qc, vi)], t_vld[h], t_osq]
                        if first:
                            dep_pv.append(t_qev)
                            if qc >= 2:
                                dep_pv.append(t_yev[qc - 2])
                                dep_pv.append(t_rbc.get(qc - 2))
                            if qc in (0, 1) and h >= 1:
                                dep_pv.append(t_attn_done[h - 1])
                        P.do("tensor", lambda t, par=par, kt=kt, qcp=qcp, qoff=qoff, first=first, is_last=is_last, i2=i2: t.matmul(
                            ps_b[:, 4 + qcp, qoff:512],
                            v_sb[:, par, kt, :], P_sb[:, i2, qoff:512],
                            start=first, stop=is_last),
                            after=dep_pv, sem=None)
                        t_den[(qc, vi)] = P.do("tensor", lambda t, qcp=qcp, qoff=qoff, first=first, is_last=is_last, i2=i2: t.matmul(
                            ps_b[:, 6 + qcp, qoff:512], ones_sq[:, :],
                            P_sb[:, i2, qoff:512],
                            start=first, stop=is_last),
                            after=dep_pv, sem="tt")

                    for qc in range(NQC):
                        qcp = qc % 2
                        kts = list(range(0, 4 * qc + 4)) + \
                              list(range(NS, NS + 4 * qc + 4))
                        nk = len(kts)
                        for vi in range(nk):
                            emit_S(qc, vi, kts, nk)
                            if vi >= LAG:
                                emit_PV(qc, vi - LAG, kts, nk, h, par)
                        for vi in range(nk - LAG, nk):
                            emit_PV(qc, vi, kts, nk, h, par)
                        t_dlast = t_den[(qc, nk - 1)]
                        dep_rec = [t_dlast]
                        if qc >= 2:
                            dep_rec.append(t_rbc[qc - 2])
                        t_rec = P.do("vector", lambda v, qcp=qcp: v.reciprocal(
                            rb_sb[:, :], ps_b[:, 6 + qcp, :]),
                            after=dep_rec, sem="vv")
                        dep_ye = [t_rec, t_dlast]
                        if qc == 0 and h >= 1:
                            dep_ye.append(t_ytst[h - 1])
                        t_yev[qc] = P.do("vector", lambda v, qc=qc, qcp=qcp: v.tensor_tensor(
                            yT_sb[:, qc * 512:(qc + 1) * 512],
                            ps_b[:, 4 + qcp, :], rb_sb[:, :],
                            ALU.mult), after=dep_ye, sem="vv")
                        t_rbc[qc] = t_yev[qc]
                    t_attn_done[h] = t_yev[NQC - 1]
                    t_exp_last[h] = t_exp[(NQC - 1, 8 * NQC - 1)]
                    t_ytst[h] = P.do("sync", lambda s, h=h: s.dma_start(
                        out=yt_hb[h, :, :], in_=yT_sb[:, :]),
                        after=[t for t in t_yev], sem="yt", n=16)
                t_yt_all = P.now("yt")
            P.flush(nc, sems)

        cbar = P.barrier_tokens()
        with contextlib.ExitStack() as _stk2:
            _e2 = _stk2.enter_context
            big = _e2(nc.sbuf_tensor("big", [128, KC, C], BF16))
            x1b = _e2(nc.sbuf_tensor("x1b", [128, 2, C], F32))
            x1c = _e2(nc.sbuf_tensor("x1c", [128, 2, C], F32))
            wsc_sb = _e2(nc.sbuf_tensor("wsc_sb", [128, C], F32))
            fb_sb = _e2(nc.sbuf_tensor("fb_sb", [128, C], F32))
            bo_sb = _e2(nc.sbuf_tensor("bo_sb", [1, C], F32))
            bob = _e2(nc.sbuf_tensor("bob", [128, C], F32))
            ones_f2 = _e2(nc.sbuf_tensor("ones_f2", [1, 128], F32))
            yTj = _e2(nc.sbuf_tensor("yTj", [128, 2, H, 128], BF16))
            qTj = _e2(nc.sbuf_tensor("qTj", [128, 2, KC, 256], BF16))
            xin2 = _e2(nc.sbuf_tensor("xin2", [128, 2, C], F32))
            att2 = _e2(nc.sbuf_tensor("att2", [128, 2, C], F32))
            wrk = _e2(nc.sbuf_tensor("wrk", [128, C], F32))
            qnat = _e2(nc.sbuf_tensor("qnat", [128, 2, C], BF16))
            bn6b = _e2(nc.sbuf_tensor("bn6b", [128, 2, NCC, 6], F32))
            mvb = _e2(nc.sbuf_tensor("mvb", [128, 2, 2], F32))
            d_ve = _e2(nc.sbuf_tensor("d_ve", [128, 2], F32))
            d_sd = _e2(nc.sbuf_tensor("d_sd", [128, 2], F32))
            d_rs = _e2(nc.sbuf_tensor("d_rs", [128, 2], F32))
            d_nm = _e2(nc.sbuf_tensor("d_nm", [128, 2], F32))
            d_am = _e2(nc.sbuf_tensor("d_am", [128, 1], F32))
            d_sc = _e2(nc.sbuf_tensor("d_sc", [128, 1], F32))
            d_rc = _e2(nc.sbuf_tensor("d_rc", [128, 2], F32))
            insc = _e2(nc.sbuf_tensor("insc", [128, NS], F32))
            c127 = _e2(nc.sbuf_tensor("c127", [128, 1], F32))
            c1285 = _e2(nc.sbuf_tensor("c1285", [128, 1], F32))
            r127 = _e2(nc.sbuf_tensor("r127", [128, 1], F32))
            qi32 = _e2(nc.sbuf_tensor("qi32", [128, C], mybir.dt.int32))
            ps_c = _e2(nc.psum_tensor("ps_c", [128, 8, 512], F32))
            t_wsc = P.do("sync", lambda s: s.dma_start(out=wsc_sb[:, :],
                                                       in_=wsc_b),
                         after=cbar, sem="c2", n=16)
            t_fb = P.do("sync", lambda s: s.dma_start(out=fb_sb[:, :],
                                                      in_=fbias_b),
                        after=cbar, sem="c2", n=16)
            t_bo = P.do("sync", lambda s: s.dma_start(out=bo_sb[:, :],
                                                      in_=bo_row),
                        after=cbar, sem="c2", n=16)
            t_wsc = t_bo
            t_fb = t_bo
            t_of2 = P.do("vector", lambda v: v.memset(ones_f2[:, :], 1.0),
                         after=cbar, sem="vv")
            t_bob = None
            t_c127 = P.do("vector", lambda v: v.memset(c127[:, :], 127.0),
                          after=cbar, sem="vv")
            t_c1285 = P.do("vector", lambda v: v.memset(c1285[:, :], 128.5),
                           after=cbar, sem="vv")
            t_r127 = P.do("vector", lambda v: v.reciprocal(r127[:, :],
                                                           c127[:, :]),
                          after=[t_c127], sem="vv")
            t_wo = None
            for kc in range(KC):
                t_wo = P.do("sync", lambda s, kc=kc: s.dma_start(
                    out=big[:, kc, :],
                    in_=wo_bf[kc * 128:(kc + 1) * 128, :]),
                    after=cbar + [t_woc], sem="wo", n=16)
            t_wo_all = t_wo

            stats_state = {"n": 0, "user": [None, None]}

            def stats(tok_in, src):
                pp = stats_state["n"] % 2
                stats_state["n"] += 1
                war = [stats_state["user"][pp]] if stats_state["user"][pp] else []
                dep0 = [tok_in] + war
                prev = None
                for g in range(NCC):
                    prev = P.do("vector", lambda v, pp=pp, g=g: v.bn_stats(
                        bn6b[:, pp, g, :], src[:, g * 512:(g + 1) * 512]),
                        after=dep0 + ([prev] if prev else []), sem="vv")
                t_mv = P.do("vector", lambda v, pp=pp: v.bn_aggr(
                    mvb[:, pp, :], bn6b[:, pp, :, :]),
                            after=[prev] + war, sem="vv")
                t_ve = P.do("vector", lambda v, pp=pp: v.tensor_scalar(
                    d_ve[:, pp:pp + 1], mvb[:, pp, 1:2], LN_EPS, None,
                    ALU.add), after=[t_mv] + war, sem="vv")
                t_sq = P.do("scalar", lambda sc, pp=pp: sc.activation(
                    d_sd[:, pp:pp + 1], d_ve[:, pp:pp + 1], AF.Sqrt),
                    after=[t_ve] + war, sem="ss")
                t_rs = P.do("vector", lambda v, pp=pp: v.reciprocal(
                    d_rs[:, pp:pp + 1], d_sd[:, pp:pp + 1]),
                    after=[t_sq] + war, sem="vv")
                t_nm = P.do("vector", lambda v, pp=pp: v.tensor_scalar(
                    d_nm[:, pp:pp + 1], mvb[:, pp, 0:1], d_rs[:, pp:pp + 1],
                    -1.0, ALU.mult, ALU.mult), after=[t_rs, t_mv] + war,
                    sem="vv")
                return t_nm, pp

            t_x1 = [None] * NS
            t_q8st = [None] * NS
            t_omm = [None] * NS
            t_tt_prev = [None] * NS
            t_x1st = [None] * NS
            t_oev = [None] * NS
            t_lnuser = [None] * NS
            t_ommtok = [None] * NS
            t_is_last = [None]
            t_xbl = [None] * NS
            t_xll = [None] * NS

            def c1_head(j):
                jp = j % 2
                dep_y = [t_yt_all]
                if j >= 2:
                    dep_y.append(t_omm[j - 2])
                t_yl = None
                for hq in range(max(1, H // 4)):
                    hn = min(4, H)
                    t_yl = P.do("sync", lambda s, j=j, jp=jp, hq=hq, hn=hn: s.dma_start(
                        out=yTj[:, jp, hq * hn:(hq + 1) * hn, :],
                        in_=yt_hb.rearrange("h p q -> p h q")
                        [:, hq * hn:(hq + 1) * hn, j * 128:(j + 1) * 128]),
                        after=dep_y, sem="yj%d" % jp, n=16)
                dep_x = cbar if j < 2 else [t_x1[j - 2]]
                t_xll[j] = P.do("sync", lambda s, j=j, jp=jp: s.dma_start(
                    out=xin2[:, jp, :],
                    in_=x_seq[j * 128:(j + 1) * 128, :]),
                    after=dep_x, sem="x2%s" % ("a" if jp == 0 else "b"), n=16)
                dep0 = [t_wo_all, t_yl, t_bo, t_of2]
                if j >= 2:
                    dep0.append(t_oev[j - 2])
                tmm = None
                for ncc in range(NCC):
                    tmm = P.do("tensor", lambda t, jp=jp, ncc=ncc: t.matmul(
                        ps_c[:, jp * 4 + ncc, :], ones_f2[:, :],
                        bo_sb[0:1, ncc * 512:(ncc + 1) * 512],
                        start=True, stop=False),
                        after=dep0, sem=None)
                for hc in range(H):
                    for ncc in range(NCC):
                        last = (hc == H - 1 and ncc == NCC - 1)
                        tmm = P.do("tensor", lambda t, jp=jp, hc=hc, ncc=ncc: t.matmul(
                            ps_c[:, jp * 4 + ncc, :], yTj[:, jp, hc, :],
                            big[:, hc, ncc * 512:(ncc + 1) * 512],
                            start=False, stop=(hc == H - 1)),
                            after=dep0, sem="tt" if last else None)
                t_omm[j] = tmm
                t_ommtok[j] = tmm

            def c1_tail(j):
                jp = j % 2
                psrc = ps_c[:, jp * 4:jp * 4 + NCC, :].rearrange(
                    "p a b -> p (a b)")
                t_nm, pp = stats(t_ommtok[j], psrc)
                t_ln2 = P.do("scalar", lambda sc, jp=jp, pp=pp, psrc=psrc: sc.activation(
                    wrk[:, :], psrc, AF.Identity,
                    bias=d_nm[:, pp:pp + 1], scale=d_rs[:, pp:pp + 1]),
                    after=[t_nm, t_ommtok[j]], sem="ss")
                t_oev[j] = t_ln2
                t_lnuser[j] = t_ln2
                stats_state["user"][pp] = t_ln2
                dep_x1 = [t_ln2, t_xll[j]]
                if j >= 2:
                    dep_x1.append(t_x1st[j - 2])
                t_x1[j] = P.do("vector", lambda v, j=j, jp=jp: v.tensor_tensor(
                    x1b[:, jp, :], wrk[:, :], xin2[:, jp, :], ALU.add),
                    after=dep_x1 + [t_ln2], sem="vv")
                t_x1st[j] = P.do("sync", lambda s, j=j, jp=jp: s.dma_start(
                    out=x1_h[j * 128:(j + 1) * 128, :], in_=x1b[:, jp, :]),
                    after=[t_x1[j]], sem="x1s%d" % jp, n=16)

                t_nm, pp = stats(t_x1[j], x1b[:, jp, :])
                t_ln3 = P.do("scalar", lambda sc, jp=jp, pp=pp: sc.activation(
                    wrk[:, :], x1b[:, jp, :], AF.Identity,
                    bias=d_nm[:, pp:pp + 1], scale=d_rs[:, pp:pp + 1]),
                    after=[t_nm, t_x1[j]], sem="ss")
                stats_state["user"][pp] = t_ln3
                t_am = P.do("vector", lambda v: v.tensor_reduce(
                    d_am[:, :], wrk[:, :], AX.X, ALU.max,
                    apply_absolute_value=True), after=[t_ln3], sem="vv")
                t_sc = P.do("vector", lambda v: v.tensor_scalar(
                    d_sc[:, :], d_am[:, :], Q_EPS, r127[:, :], ALU.max,
                    ALU.mult), after=[t_am, t_r127], sem="vv")
                t_is = P.do("vector", lambda v, j=j: v.tensor_copy(
                    insc[:, j:j + 1], d_sc[:, :]), after=[t_sc], sem="vv")
                t_is_last[0] = t_is
                dep_rc = [t_sc]
                if j >= 2:
                    dep_rc.append(t_tt_prev[j - 2])
                t_rc = P.do("vector", lambda v, jp=jp: v.reciprocal(
                    d_rc[:, jp:jp + 1], d_sc[:, :]), after=dep_rc, sem="vv")
                t_t = P.do("scalar", lambda sc, jp=jp: sc.activation(
                    att2[:, jp, :], wrk[:, :], AF.Identity,
                    bias=c1285[:, :], scale=d_rc[:, jp:jp + 1]),
                    after=[t_rc, t_ln3, t_c1285], sem="ss")
                t_tt_prev[j] = t_t
                t_qi = P.do("vector", lambda v, jp=jp: v.tensor_copy(
                    qi32[:, :], att2[:, jp, :]), after=[t_t], sem="vv")
                dep_q = [t_qi]
                if j >= 2:
                    dep_q.append(t_q8st[j - 2])
                t_qn = P.do("vector", lambda v, jp=jp: v.tensor_scalar(
                    qnat[:, jp, :], qi32[:, :], 128.0, None, ALU.subtract),
                    after=dep_q, sem="vv")
                t_lnuser[j] = t_qn
                stats_state["war"] = t_qn
                t_q8st[j] = P.do("sync", lambda s, j=j, jp=jp: s.dma_start(
                    out=q8_h[j * 128:(j + 1) * 128, :], in_=qnat[:, jp, :]),
                    after=[t_qn], sem="q8%d" % jp, n=16)

            for j in range(NS):
                c1_head(j)
                if j >= 1:
                    c1_tail(j - 1)
            c1_tail(NS - 1)
            t_q8_all = P.now("q80")
            t_q8_all2 = P.now("q81")

            bar2 = P.barrier_tokens() + [t_w8c]
            t_wT = None
            for kc in range(KC):
                t_wT = P.do("scalar", lambda s, kc=kc: s.dma_start_transpose(
                    out=big[:, kc, :],
                    in_=w8_bf[:, kc * 128:(kc + 1) * 128]),
                    after=bar2, sem="wt", n=16)
            t_wT_all = t_wT

            t_fmm = [None] * NS
            t_ffnmm = [None] * NS
            t_out = [None] * NS
            t_resid = [None] * NS
            t_ql_pair = [None]

            def c2_head(j):
                jp = j % 2
                if j % 2 == 0:
                    sl = (j // 2) % 2
                    dep_ql = [t_q8st[j], t_q8st[j + 1]]
                    if j >= 4:
                        dep_ql.append(t_ffnmm[j - 3])
                    t_ql = None
                    for kc in range(KC):
                        t_ql = P.do("scalar", lambda s, j=j, sl=sl, kc=kc: s.dma_start_transpose(
                            out=qTj[:, sl, kc, :],
                            in_=q8_h[j * 128:(j + 2) * 128,
                                     kc * 128:(kc + 1) * 128]),
                            after=dep_ql, sem="qj%d" % sl, n=16)
                    t_ql_pair[0] = t_ql
                t_ql = t_ql_pair[0]
                dep_xb = [t_q8_all, t_q8_all2, P.now("x1s0"), P.now("x1s1")]
                if j >= 2:
                    dep_xb.append(t_resid[j - 2])
                t_xbl[j] = P.do("sync", lambda s, j=j, jp=jp: s.dma_start(
                    out=x1c[:, jp, :], in_=x1_h[j * 128:(j + 1) * 128, :]),
                    after=dep_xb, sem="xc%d" % jp, n=16)
                dep0 = [t_ql, t_wT_all]
                if j >= 2:
                    dep0.append(t_fmm[j - 2])
                tmm = None
                sl2, dd = (j // 2) % 2, j % 2
                for kc in range(KC):
                    for ncc in range(NCC):
                        last = (kc == KC - 1 and ncc == NCC - 1)
                        tmm = P.do("tensor", lambda t, jp=jp, kc=kc, ncc=ncc, sl2=sl2, dd=dd: t.matmul(
                            ps_c[:, jp * 4 + ncc, :],
                            qTj[:, sl2, kc, dd * 128:(dd + 1) * 128],
                            big[:, kc, ncc * 512:(ncc + 1) * 512],
                            start=(kc == 0), stop=(kc == KC - 1)),
                            after=dep0, sem="tt" if last else None)
                t_ffnmm[j] = tmm

            def c2_tail(j):
                jp = j % 2
                tmm = t_ffnmm[j]
                dep_s1 = [tmm, t_is_last[0], t_wsc]
                if j >= 2:
                    dep_s1.append(t_resid[j - 2])
                    dep_s1.append(t_out[j - 2])
                t_s1 = P.do("scalar", lambda sc, j=j, jp=jp: sc.activation(
                    att2[:, jp, :], ps_c[:, jp * 4:jp * 4 + NCC, :].rearrange(
                        "p a b -> p (a b)"), AF.Identity,
                    bias=0.0, scale=insc[:, j:j + 1]),
                    after=dep_s1, sem="ss")
                t_fmm[j] = t_s1
                t_s2 = P.do("gpsimd", lambda g, jp=jp: g.tensor_tensor(
                    att2[:, jp, :], att2[:, jp, :], wsc_sb[:, :], ALU.mult),
                    after=[t_s1, t_wsc], sem="gg")
                t_s3 = P.do("gpsimd", lambda g, jp=jp: g.tensor_tensor(
                    att2[:, jp, :], att2[:, jp, :], fb_sb[:, :], ALU.add),
                    after=[t_s2, t_fb], sem="gg")
                t_nm, pp = stats(t_s3, att2[:, jp, :])
                t_ln4 = P.do("scalar", lambda sc, jp=jp, pp=pp: sc.activation(
                    wrk[:, :], att2[:, jp, :], AF.Identity,
                    bias=d_nm[:, pp:pp + 1], scale=d_rs[:, pp:pp + 1]),
                    after=[t_nm, t_s3], sem="ss")
                t_resid[j] = P.do("vector", lambda v, jp=jp: v.tensor_tensor(
                    att2[:, jp, :], wrk[:, :], x1c[:, jp, :], ALU.add),
                    after=[t_ln4, t_xbl[j]], sem="vv")
                stats_state["user"][pp] = t_resid[j]
                t_out[j] = P.do("sync", lambda s, j=j, jp=jp: s.dma_start(
                    out=out_t[j * 128:(j + 1) * 128, :], in_=att2[:, jp, :]),
                    after=[t_resid[j]], sem="out%d" % jp, n=16)

            for j in range(NS):
                c2_head(j)
                if j >= 1:
                    c2_tail(j - 1)
            c2_tail(NS - 1)
            P.do("sync", None, after=[t_out[NS - 2], t_out[NS - 1]])
            P.flush(nc, sems)

    return nc


def make_in_maps(x, Wq, bq, Wk, bk, Wv, bv, Wo, bo, w8, w_scale, fbias,
                 T=4096, C=2048, H=16):
    D = 128
    NS = (T // 128) // 2
    bf = ml_dtypes.bfloat16
    B = x.shape[0]

    wq_r = np.ascontiguousarray(Wq.transpose(1, 0, 2).reshape(C, H * D))
    wk_r = np.ascontiguousarray(Wk.transpose(1, 0, 2).reshape(C, H * D))
    wv_r = np.ascontiguousarray(Wv.transpose(1, 0, 2).reshape(C, H * D))
    wo = np.ascontiguousarray(Wo)
    bqk = np.ascontiguousarray(np.concatenate([bq.T, bk.T], axis=1))
    bv_row = np.ascontiguousarray(bv.reshape(1, H * D))
    bo_row = np.ascontiguousarray(bo.reshape(1, C))
    w8c = np.ascontiguousarray(w8)
    wsc_b = np.ascontiguousarray(
        np.broadcast_to(w_scale.reshape(1, C), (128, C)))
    fb_b = np.ascontiguousarray(
        np.broadcast_to(fbias.reshape(1, C), (128, C)))

    tri = np.tril(np.ones((128, 128), np.float32)).T
    zeros = np.zeros((128, 128), np.float32)
    ones = np.ones((128, 128), np.float32)

    in_maps = []
    for core in range(2 * B):
        b, h = divmod(core, 2)
        perm = [2 * i + h for i in range(NS)] + \
               [2 * i + (1 - h) for i in range(NS)]
        rows = np.concatenate([np.arange(t0 * 128, (t0 + 1) * 128)
                               for t0 in perm])
        x_core = np.ascontiguousarray(x[b][rows])
        m = np.empty((128, NS * 2 * 128), np.float32)
        for i in range(NS):
            m[:, (2 * i) * 128:(2 * i + 1) * 128] = tri
            m[:, (2 * i + 1) * 128:(2 * i + 2) * 128] = \
                zeros if h == 0 else ones
        in_maps.append({
            "x_seq": x_core,
            "wq_r": wq_r, "wk_r": wk_r, "wv_r": wv_r, "wo_in": wo,
            "bqk_cols": bqk, "bv_row": bv_row, "bo_row": bo_row,
            "w8": w8c, "wsc_b": wsc_b, "fbias_b": fb_b,
            "masks_in": m.astype(bf),
        })
    return in_maps


def assemble_out(results, B, T=4096, C=2048):
    NS = (T // 128) // 2
    out = np.empty((B, T, C), np.float32)
    for core in range(2 * B):
        b, h = divmod(core, 2)
        o = np.asarray(results[core]["out"])
        for i in range(NS):
            g = 2 * i + h
            out[b, g * 128:(g + 1) * 128, :] = o[i * 128:(i + 1) * 128, :]
    return out


_NC_CACHE = {}


def kernel(L0_attn_x, L0_attn_Wq, L0_attn_bq, L0_attn_Wk, L0_attn_bk,
           L0_attn_Wv, L0_attn_bv, L0_attn_Wo, L0_attn_bo,
           L0_ffn_weight, L0_ffn_w_scale, L0_ffn_bias):
    x = np.asarray(L0_attn_x)
    B = x.shape[0]
    in_maps = make_in_maps(
        x, np.asarray(L0_attn_Wq), np.asarray(L0_attn_bq),
        np.asarray(L0_attn_Wk), np.asarray(L0_attn_bk),
        np.asarray(L0_attn_Wv), np.asarray(L0_attn_bv),
        np.asarray(L0_attn_Wo), np.asarray(L0_attn_bo),
        np.asarray(L0_ffn_weight), np.asarray(L0_ffn_w_scale),
        np.asarray(L0_ffn_bias))
    if "nc" not in _NC_CACHE:
        _NC_CACHE["nc"] = build_nc()
    res = run_bass_kernel_spmd(_NC_CACHE["nc"], in_maps,
                               core_ids=list(range(2 * B)))
    return assemble_out(res.results, B)
